# version 14
# baseline (speedup 1.0000x reference)
import contextlib
import os
import numpy as np
import concourse.bass as bass
import concourse.mybir as mybir
from concourse.bass_utils import run_bass_kernel_spmd

F32 = mybir.dt.float32
BF16 = mybir.dt.bfloat16
AF = mybir.ActivationFunctionType
ALU = mybir.AluOpType
AX = mybir.AxisListType.X

C23 = float(3 * 2.0**22)
C52 = float(3 * 2.0**51)
C36 = float(3 * 2.0**35)
C37 = float(3 * 2.0**36)

T = 2048
D = 1024
NSEQ = 2
NTILE = 16
NTG = 4
NHG = 8
FF = 4096

ENGINES = ("sync", "gpsimd", "tensor", "vector", "scalar")


class Prog:
    def __init__(self, nc):
        self.nc = nc
        self.ops = {e: [] for e in ENGINES}
        self.cnt = {}
        self.seen = {e: {} for e in ENGINES}
        self.sems = {}

    def make_sems(self, stack):
        for n in ["d_sync", "d_gpsimd", "m_tensor", "m_vector", "m_scalar",
                  "m_gpsimd"]:
            self.sems[n] = stack.enter_context(self.nc.semaphore(n))
            self.cnt[n] = 0

    def op(self, eng, fn):
        self.ops[eng].append(fn)

    def dma(self, q, out, in_):
        s = "d_" + q
        self.cnt[s] += 16
        v = self.cnt[s]
        sem = self.sems[s]

        def f(e, out=out, in_=in_, sem=sem):
            e.dma_start(out=out, in_=in_).then_inc(sem, 16)
        self.op(q, f)
        return (s, v)

    def wait(self, eng, tok):
        if tok is None:
            return
        s, v = tok
        if self.seen[eng].get(s, 0) >= v:
            return
        self.seen[eng][s] = v
        sem = self.sems[s]
        self.op(eng, lambda e, sem=sem, v=v: e.wait_ge(sem, v))

    def waits(self, eng, toks):
        for t in toks:
            self.wait(eng, t)

    def mark(self, eng):
        s = "m_" + eng
        self.cnt[s] += 1
        v = self.cnt[s]
        sem = self.sems[s]
        self.op(eng, lambda e, sem=sem: e.maybe_drain_then_inc((sem, 1)))
        return (s, v)

    def mark_inst(self, eng, fn):
        s = "m_" + eng
        self.cnt[s] += 1
        v = self.cnt[s]
        sem = self.sems[s]

        def f(e, fn=fn, sem=sem):
            fn(e).then_inc(sem, 1)
        self.op(eng, f)
        return (s, v)

    def t(self, fn):
        self.op("tensor", fn)

    def v(self, fn):
        self.op("vector", fn)

    def s(self, fn):
        self.op("scalar", fn)

    def g(self, fn):
        self.op("gpsimd", fn)

    def drain(self, eng):
        self.op(eng, lambda e: e.drain())

    def rnd(self, out, in_, t1, t2, cbig=None):
        nflag = self.nflag_col

        def f(e, out=out, in_=in_, t1=t1, t2=t2, nflag=nflag):
            e.tensor_scalar(out=t1, in0=in_, scalar1=C23, scalar2=C23,
                            op0=ALU.add, op1=ALU.subtract)
            e.tensor_scalar(out=t2, in0=in_, scalar1=C52, scalar2=C52,
                            op0=ALU.add, op1=ALU.subtract)
            e.scalar_tensor_tensor(out=t2, in0=in_, scalar=nflag, in1=t2,
                                   op0=ALU.mult, op1=ALU.add)
            e.tensor_tensor(out=out, in0=t1, in1=t2, op=ALU.add)
        self.op("vector", f)

    def split(self, src, hi, lo, bias=None):
        if bias is None:
            self.s(lambda e, src=src, hi=hi: e.activation(
                out=hi, in_=src, func=AF.Identity))
        else:
            self.s(lambda e, src=src, hi=hi, bias=bias: e.activation(
                out=hi, in_=src, func=AF.Identity, bias=bias))
        ms = self.mark("scalar")
        self.wait("vector", ms)
        if bias is None:
            self.v(lambda e, src=src, hi=hi, lo=lo: e.scalar_tensor_tensor(
                out=lo, in0=src, scalar=1.0, in1=hi,
                op0=ALU.mult, op1=ALU.subtract))
        else:
            self.v(lambda e, src=src, hi=hi, lo=lo, bias=bias:
                   e.scalar_tensor_tensor(out=lo, in0=src, scalar=bias,
                                          in1=hi, op0=ALU.add,
                                          op1=ALU.subtract))

    def emit(self, block):
        for eng in ENGINES:
            ops = self.ops[eng]
            deco = getattr(block, eng)

            def body(e, ops=ops):
                for f in ops:
                    f(e)
            deco(body)


def build():
    nc = bass.Bass("TRN2", target_bir_lowering=False,
                   detect_race_conditions=False)
    stack = contextlib.ExitStack()
    ap = lambda h: h.ap()

    x_in = ap(nc.dram_tensor("x", [NSEQ, T, D], F32, kind="ExternalInput"))
    wqkvT = ap(nc.dram_tensor("wqkvT", [D, 3 * D], F32, kind="ExternalInput"))
    battn = ap(nc.dram_tensor("battn", [128, 24], F32, kind="ExternalInput"))
    laTa = ap(nc.dram_tensor("laTa", [128, 256], F32, kind="ExternalInput"))
    lbTa = ap(nc.dram_tensor("lbTa", [32, 3 * D], F32, kind="ExternalInput"))
    wprojT = ap(nc.dram_tensor("wprojT", [D, D], F32, kind="ExternalInput"))
    bproj = ap(nc.dram_tensor("bproj", [128, 8], F32, kind="ExternalInput"))
    laTp = ap(nc.dram_tensor("laTp", [128, 256], F32, kind="ExternalInput"))
    lbTp = ap(nc.dram_tensor("lbTp", [32, D], F32, kind="ExternalInput"))
    w1 = ap(nc.dram_tensor("w1", [D, FF], F32, kind="ExternalInput"))
    b1c = ap(nc.dram_tensor("b1c", [128, 32], F32, kind="ExternalInput"))
    w2 = ap(nc.dram_tensor("w2", [FF, D], F32, kind="ExternalInput"))
    b2c = ap(nc.dram_tensor("b2c", [128, 8], F32, kind="ExternalInput"))
    identx = ap(nc.dram_tensor("identx", [128, 128], F32, kind="ExternalInput"))
    maskx = ap(nc.dram_tensor("maskx", [128, 896], BF16, kind="ExternalInput"))
    cvec = ap(nc.dram_tensor("cvec", [128, 8], F32, kind="ExternalInput"))
    out_hbm = ap(nc.dram_tensor("out", [NSEQ, T, D], F32,
                                kind="ExternalOutput"))

    wqkv_h = ap(nc.dram_tensor("wqkv_h", [24, 8, 128, 128], BF16))
    wqkv_l = ap(nc.dram_tensor("wqkv_l", [24, 8, 128, 128], BF16))
    wproj_h = ap(nc.dram_tensor("wproj_h", [8, 8, 128, 128], BF16))
    wproj_l = ap(nc.dram_tensor("wproj_l", [8, 8, 128, 128], BF16))
    w1_hs = ap(nc.dram_tensor("w1_hs", [32, 8, 128, 128], BF16))
    w1_ls = ap(nc.dram_tensor("w1_ls", [32, 8, 128, 128], BF16))
    w2_hs = ap(nc.dram_tensor("w2_hs", [8, 32, 128, 128], BF16))
    w2_ls = ap(nc.dram_tensor("w2_ls", [8, 32, 128, 128], BF16))

    sbuf = lambda nm, sh, dt=F32: nc.alloc_sbuf_tensor(nm, sh, dt).ap()
    B1 = sbuf("B1", [128, 32768], BF16)
    B2 = sbuf("B2", [128, 32768], BF16)
    B3 = sbuf("B3", [128, 14336], BF16)
    F1 = sbuf("F1", [128, 8192], F32)

    xhat_h = B1[:, 0:16384]
    xhat_l = B1[:, 16384:32768]
    x1h = B1[:, 0:16384]
    x1l = B1[:, 16384:32768]

    y_h = B2[:, 0:16384]
    y_l = B2[:, 16384:32768]
    hgel_h = B2[:, 0:16384]
    hgel_l = B2[:, 16384:32768]
    wsp_h = B2[:, 0:2048]
    wsp_l = B2[:, 2048:4096]
    u_l = B2[0:32, 14336:16384]
    u_h = B2[0:32, 30720:32768]

    qg_h = B3[:, 0:2048]
    qg_l = B3[:, 2048:4096]
    kg_h = B3[:, 4096:6144]
    kg_l = B3[:, 6144:8192]
    vt_h = B3[:, 8192:10240]
    vt_l = B3[:, 10240:12288]
    pt_h = B3[:, 12288:13312]
    pt_l = B3[:, 13312:14336]
    xh1_h = B3[:, 0:4096]
    xh1_l = B3[:, 4096:8192]
    u2_h = B3[0:32, 12288:12800]
    u2_l = B3[0:32, 12800:13312]

    wsplit = F1[:, 0:2048]
    xin = F1[:, 0:2048]
    xh_tm = F1[:, 2048:3072]
    sq_scr1 = F1[:, 3072:4096]
    sq_ln = sq_scr1
    vstage = F1[:, 4096:4608]
    ef32 = [F1[:, 4608 + i * 512: 4608 + (i + 1) * 512] for i in range(2)]
    rb64 = F1[:, 5632:6144]
    ystage = F1[:, 6144:6656]
    rrow = F1[0:1, 6656:7168]
    pstage = F1[:, 0:4096]
    x1stg = F1[:, 4096:5120]
    ustage = F1[0:32, 5120:5632]
    xres = F1[:, 5632:7680]
    GB = [F1[:, i * 512: (i + 1) * 512] for i in range(8)]
    GB2 = [F1[:, 4096 + i * 512: 4096 + (i + 1) * 512]
           for i in range(8)]
    ot_st = F1[:, 0:2048]
    xfull = F1[:, 0:1024]
    ofm = F1[:, 4096:8192]

    ws_h = sbuf("ws_h", [128, 2048], BF16)
    ws_l = sbuf("ws_l", [128, 2048], BF16)

    identt = sbuf("identt", [128, 128])
    maskt = sbuf("maskt", [128, 896], BF16)
    cv = sbuf("cv", [128, 8])
    bat = sbuf("bat", [128, 24])
    bpt = sbuf("bpt", [128, 8])
    b1t = sbuf("b1t", [128, 32])
    b2t = sbuf("b2t", [128, 8])
    la_h = sbuf("la_h", [128, 256], BF16)
    la_l = sbuf("la_l", [128, 256], BF16)
    lap_h = sbuf("lap_h", [128, 256], BF16)
    lap_l = sbuf("lap_l", [128, 256], BF16)
    lbp_h = sbuf("lbp_h", [32, D], BF16)
    lbp_l = sbuf("lbp_l", [32, D], BF16)
    lbs_h = sbuf("lbs_h", [32, 256], BF16)
    lbs_l = sbuf("lbs_l", [32, 256], BF16)
    lbstg = sbuf("lbstg", [32, 256])
    s1c = sbuf("s1c", [128, 16])
    s2c = sbuf("s2c", [128, 16])
    mcol = sbuf("mcol", [128, 16])
    vcol = sbuf("vcol", [128, 16])
    rstd = sbuf("rstd", [128, 16])
    rstd0 = sbuf("rstd0", [128, 16])
    vcol0 = sbuf("vcol0", [128, 16])
    mbias = sbuf("mbias", [128, 16])
    ones_bf = sbuf("ones_bf", [128, 1], BF16)
    ones_row = sbuf("ones_row", [1, 128], F32)
    fcol = sbuf("fcol", [128, 4], F32)

    pball = lambda nm: nc.alloc_psum_tensor(nm, [128, 512], F32).ap()
    pA = [pball("pA0"), pball("pA1")]
    pT = [pball("pT0"), pball("pT1")]
    pS = [pball("pS0"), pball("pS1")]
    pY = [pball("pY0"), pball("pY1")]

    P = Prog(nc)
    P.make_sems(stack)
    CV_Y1, CV_EXP, CV_EPS, CV_NT = 0, 1, 2, 3
    DBG = bool(os.environ.get("KDBG"))
    PHASES = int(os.environ.get("KPHASE", "99"))

    def dbgt(name, src_ap, toks):
        if not DBG:
            return
        t = nc.dram_tensor("dbg_" + name, list(src_ap.shape), src_ap.dtype,
                           kind="ExternalOutput").ap()
        P.waits("sync", list(toks))
        P.dma("sync", t, src_ap)

    x_r = x_in.rearrange("s (n p) d -> s n p d", p=128)
    out_r = out_hbm.rearrange("s (n p) d -> s n p d", p=128)

    tok_consts = [P.dma("sync", identt, identx), P.dma("sync", maskt, maskx),
                  P.dma("sync", cv, cvec), P.dma("sync", bat, battn),
                  P.dma("sync", bpt, bproj)]
    tok_b = [P.dma("sync", b1t, b1c), P.dma("sync", b2t, b2c)]
    P.waits("vector", tok_b)
    P.v(lambda e: e.tensor_scalar(out=b1t, in0=b1t, scalar1=256.0,
                                  scalar2=None, op0=ALU.mult))
    P.v(lambda e: e.tensor_scalar(out=b2t, in0=b2t, scalar1=256.0,
                                  scalar2=None, op0=ALU.mult))
    P.v(lambda e: e.memset(ones_bf, 1.0))
    P.v(lambda e: e.memset(ones_row, 1.0))
    P.v(lambda e: e.memset(fcol[:, 0:1], 1.0))
    P.drain("vector")
    P.v(lambda e: e.tensor_scalar(out=fcol[:, 1:2], in0=fcol[:, 0:1],
                                  scalar1=float(2.0**-40), scalar2=1.0,
                                  op0=ALU.add, op1=ALU.subtract))
    P.drain("vector")
    P.v(lambda e: e.tensor_scalar(out=fcol[:, 2:3], in0=fcol[:, 1:2],
                                  scalar1=float(2.0**40), scalar2=None,
                                  op0=ALU.mult))
    P.drain("vector")
    P.v(lambda e: e.tensor_scalar(out=fcol[:, 3:4], in0=fcol[:, 2:3],
                                  scalar1=-1.0, scalar2=None, op0=ALU.mult))
    P.drain("vector")
    P.flag_col = fcol[:, 2:3]
    P.nflag_col = fcol[:, 3:4]

    prev_split = [None]

    def beat():
        P.t(lambda e: e.matmul(pY[0][0:1, 0:1], ones_bf, ones_bf,
                               start=True, stop=True))

    def split_to_hbm(src, hi, lo, rows, cols):
        for r0 in range(0, rows, 128):
            ci = r0 // 128
            c = 0
            while c < cols:
                w = min(2048, cols - c)
                nb = w // 128
                o0 = c // 128
                t_in = P.dma("sync", wsplit[:, 0:w], src[r0:r0 + 128, c:c + w])
                P.waits("vector", [t_in, prev_split[0]])
                P.waits("scalar", [t_in, prev_split[0]])
                P.split(wsplit[:, 0:w], wsp_h[:, 0:w], wsp_l[:, 0:w])
                mv = P.mark("vector")
                P.wait("tensor", t_in)
                beat()
                P.wait("tensor", mv)
                beat()
                P.wait("sync", mv)
                srch = wsp_h[:, 0:w].rearrange("p (o x) -> p o x", o=nb)
                srcl = wsp_l[:, 0:w].rearrange("p (o x) -> p o x", o=nb)
                dsth = hi[o0:o0 + nb, ci].rearrange("o p x -> p o x")
                dstl = lo[o0:o0 + nb, ci].rearrange("o p x -> p o x")
                P.dma("sync", dsth, srch)
                prev_split[0] = P.dma("sync", dstl, srcl)
                c += w

    split_to_hbm(wqkvT, wqkv_h, wqkv_l, D, 3 * D)
    split_to_hbm(wprojT, wproj_h, wproj_l, D, D)
    split_to_hbm(w1, w1_hs, w1_ls, D, FF)
    split_to_hbm(w2, w2_hs, w2_ls, FF, D)

    def load_split_small(src, hi, lo, prow, cols):
        t_in = P.dma("sync", wsplit[0:prow, 0:cols], src)
        P.waits("scalar", [t_in, prev_split[0]])
        P.waits("vector", [t_in, prev_split[0]])
        P.split(wsplit[0:prow, 0:cols], hi, lo)
        mv = P.mark("vector")
        P.wait("sync", mv)
        prev_split[0] = mv
        return mv

    load_split_small(laTa, la_h, la_l, 128, 256)
    load_split_small(laTp, lap_h, lap_l, 128, 256)
    tok_small = load_split_small(lbTp, lbp_h, lbp_l, 32, D)
    P.waits("tensor", tok_consts + [tok_small])
    P.waits("scalar", tok_consts)
    P.waits("vector", tok_consts)
    P.waits("gpsimd", tok_consts)

    ws_state = {"idx": 0, "guard": [None, None]}

    def ws_load(srcs):
        h = ws_state["idx"] % 2
        ws_state["idx"] += 1
        P.wait("sync", ws_state["guard"][h])
        P.wait("gpsimd", ws_state["guard"][h])
        off = h * 1024
        tks = []
        pos = 0
        for cols, sh, sl, nch in srcs:
            dsth = ws_h[:, off + pos: off + pos + cols].rearrange(
                "p (c o) -> p c o", c=nch)
            dstl = ws_l[:, off + pos: off + pos + cols].rearrange(
                "p (c o) -> p c o", c=nch)
            tks.append(P.dma("sync", dsth, sh.rearrange("c p o -> p c o")))
            tks.append(P.dma("sync", dstl, sl.rearrange("c p o -> p c o")))
            pos += cols
        return h, tks

    def ws_done(h, tok):
        ws_state["guard"][h] = tok

    lb_state = {"idx": 0, "guard": [None, None]}

    def lb_load(col):
        h = lb_state["idx"] % 2
        lb_state["idx"] += 1
        P.wait("sync", lb_state["guard"][h])
        tk = P.dma("sync", lbstg[:, h * 128:(h + 1) * 128],
                   lbTa[:, col:col + 128])
        P.waits("scalar", [tk])
        P.waits("vector", [tk])
        P.split(lbstg[:, h * 128:(h + 1) * 128],
                lbs_h[:, h * 128:(h + 1) * 128],
                lbs_l[:, h * 128:(h + 1) * 128])
        return h, P.mark("vector")

    def lb_done(h, tok):
        lb_state["guard"][h] = tok

    def wslice(hr, lr, c0, c1, oblk):
        return [((c1 - c0) * 128, hr[oblk, c0:c1], lr[oblk, c0:c1], c1 - c0)]

    tok_seq_end = None
    tok_out_prev = None

    tok_p0_out = prev_split[0]
    for s in range(NSEQ):
        P.wait("scalar", tok_p0_out)
        P.wait("vector", tok_p0_out)
        P.wait("gpsimd", tok_p0_out)
        norm_done = [None, None]
        ptev = [None, None]
        for t in range(NTILE):
            sl = (t % 2) * 1024
            P.wait("gpsimd", norm_done[t % 2])
            if s > 0:
                P.wait("gpsimd", tok_seq_end)
                P.wait("gpsimd", tok_out_prev)
            tok_x = P.dma("gpsimd", xin[:, sl:sl + 1024], x_r[s, t])
            if s == 0 and t == 0:
                x_toks_dbg = tok_x
            P.wait("scalar", tok_x)
            P.wait("vector", tok_x)
            if s > 0:
                P.waits("scalar", [tok_out_prev])
                P.waits("vector", [tok_out_prev])
            P.s(lambda e, sl=sl, t=t: e.activation(
                out=sq_scr1, in_=xin[:, sl:sl + 1024],
                func=AF.Square, accum_out=s2c[:, t:t + 1]))
            P.v(lambda e, sl=sl, t=t: e.reduce_sum(
                out=s1c[:, t:t + 1], in_=xin[:, sl:sl + 1024], axis=AX))
            ms = P.mark("scalar")
            P.wait("tensor", ms)
            beat()
            P.wait("vector", ms)
            P.drain("vector")
            P.v(lambda e, t=t: e.tensor_scalar(
                out=mcol[:, t:t + 1], in0=s1c[:, t:t + 1], scalar1=1.0 / D,
                scalar2=None, op0=ALU.mult))
            P.drain("vector")
            P.v(lambda e, t=t: e.tensor_tensor(
                out=vcol0[:, t:t + 1], in0=mcol[:, t:t + 1],
                in1=mcol[:, t:t + 1], op=ALU.mult))
            P.drain("vector")
            P.v(lambda e, t=t: e.scalar_tensor_tensor(
                out=vcol[:, t:t + 1], in0=s2c[:, t:t + 1], scalar=1.0 / D,
                in1=vcol0[:, t:t + 1], op0=ALU.mult, op1=ALU.subtract))
            mv = P.mark("vector")
            P.wait("tensor", mv)
            beat()
            P.wait("scalar", mv)
            P.s(lambda e, t=t: e.activation(
                out=rstd0[:, t:t + 1], in_=vcol[:, t:t + 1], func=AF.Sqrt,
                bias=cv[:, CV_EPS:CV_EPS + 1]))
            ms2 = P.mark("scalar")
            P.wait("vector", ms2)
            P.v(lambda e, t=t: e.reciprocal(out=rstd[:, t:t + 1],
                                            in_=rstd0[:, t:t + 1]))
            P.drain("vector")
            P.v(lambda e, t=t: e.scalar_tensor_tensor(
                out=mbias[:, t:t + 1], in0=mcol[:, t:t + 1], scalar=-1.0,
                in1=rstd[:, t:t + 1], op0=ALU.mult, op1=ALU.mult))
            mv2 = P.mark("vector")
            P.wait("tensor", mv2)
            beat()
            P.wait("scalar", mv2)
            P.s(lambda e, sl=sl, t=t: e.activation(
                out=xh_tm, in_=xin[:, sl:sl + 1024], func=AF.Identity,
                scale=rstd[:, t:t + 1], bias=mbias[:, t:t + 1]))
            norm_done[t % 2] = P.mark("scalar")
            if s == 0 and t == 0:
                dbgt("xin0", xin[:, 0:1024], [x_toks_dbg])
                dbgt("stats", s1c, [norm_done[0]])
                dbgt("stats2", s2c, [norm_done[0]])
                dbgt("rstd", rstd, [norm_done[0]])
                dbgt("mcol", mcol, [norm_done[0]])
                dbgt("vcol", vcol, [norm_done[0]])
                dbgt("rstd0", rstd0, [norm_done[0]])
                dbgt("xhtm0", xh_tm, [norm_done[0]])
            P.wait("tensor", norm_done[t % 2])
            for half in range(2):
                P.wait("tensor", ptev[half])
                for j in range(4):
                    c = half * 4 + j
                    P.t(lambda e, half=half, j=j, c=c: e.transpose(
                        pT[half][:, j * 128:(j + 1) * 128],
                        xh_tm[:, c * 128:(c + 1) * 128], identt))
            mt = P.mark("tensor")
            P.wait("scalar", mt)
            P.wait("vector", mt)
            for half in range(2):
                c0 = half * 4
                dsth = xhat_h.rearrange("p (c w) -> p c w", c=8)[
                    :, c0:c0 + 4, t * 128:(t + 1) * 128]
                dstl = xhat_l.rearrange("p (c w) -> p c w", c=8)[
                    :, c0:c0 + 4, t * 128:(t + 1) * 128]
                ptr = pT[half].rearrange("p (c w) -> p c w", c=4)
                P.split(ptr, dsth, dstl)
                ptev[half] = P.mark("vector")
        tok_xhat = P.mark("vector")
        if s == 0:
            dbgt("xhat_h", xhat_h[:, 0:2048], [tok_xhat, P.mark("scalar")])
            dbgt("xhat_l", xhat_l[:, 0:2048], [tok_xhat])
        if PHASES <= 1:
            break

        P.waits("tensor", [tok_xhat])
        ev_prev = [None, None]
        for tg in range(NTG):
            ab = tg % 2
            P.wait("tensor", ev_prev[ab])

            def u_mms(e, tg=tg, ab=ab):
                last = None
                for c in range(8):
                    wh = la_h[:, c * 32:(c + 1) * 32]
                    wl = la_l[:, c * 32:(c + 1) * 32]
                    xh = xhat_h[:, c * 2048 + tg * 512:
                                c * 2048 + (tg + 1) * 512]
                    xl = xhat_l[:, c * 2048 + tg * 512:
                                c * 2048 + (tg + 1) * 512]
                    e.matmul(pA[ab][0:32, :], wh, xh, start=(c == 0),
                             stop=False)
                    e.matmul(pA[ab][0:32, :], wh, xl, start=False, stop=False)
                    last = e.matmul(pA[ab][0:32, :], wl, xh, start=False,
                                    stop=(c == 7))
                return last
            mm_tok = P.mark_inst("tensor", u_mms)
            P.wait("scalar", mm_tok)
            P.s(lambda e, ab=ab: e.activation(
                out=ustage, in_=pA[ab][0:32, :], func=AF.Identity, scale=0.5))
            ms = P.mark("scalar")
            P.wait("vector", ms)
            P.v(lambda e, tg=tg: e.tensor_copy(
                out=u_h[:, tg * 512:(tg + 1) * 512], in_=ustage))
            P.v(lambda e, tg=tg: e.scalar_tensor_tensor(
                out=u_l[:, tg * 512:(tg + 1) * 512], in0=ustage, scalar=1.0,
                in1=u_h[:, tg * 512:(tg + 1) * 512], op0=ALU.mult,
                op1=ALU.subtract))
            ev_prev[ab] = P.mark("vector")
        tok_u = P.mark("vector")
        if s == 0:
            dbgt("u_h", u_h, [tok_u])
            dbgt("u_l", u_l, [tok_u])
        if PHASES <= 2:
            break

        pair_exp = None
        pv_prev = None
        sp_prev = None
        ynorm_done = [None, None]
        for g in range(NHG):
            P.waits("tensor", [tok_xhat, tok_u])
            P.wait("tensor", ynorm_done[0])
            P.wait("tensor", ynorm_done[1])
            qkv_names = [(0, qg_h, qg_l), (1, kg_h, kg_l), (2, None, None)]
            for pi in range(3):
                wcol = pi * D + g * 128
                wh_half, wtok = ws_load(
                    wslice(wqkv_h, wqkv_l, 0, 8, wcol // 128))
                lb_half, lbtok = lb_load(wcol)
                P.waits("tensor", wtok + [lbtok])
                last_mm = None
                for tg in range(NTG):
                    ab = tg % 2
                    P.wait("tensor", ev_prev[ab])

                    def qkv_mms(e, wh_half=wh_half, lb_half=lb_half, tg=tg,
                                ab=ab):
                        off = wh_half * 1024
                        for c in range(8):
                            wh = ws_h[:, off + c * 128: off + (c + 1) * 128]
                            wl = ws_l[:, off + c * 128: off + (c + 1) * 128]
                            xh = xhat_h[:, c * 2048 + tg * 512:
                                        c * 2048 + (tg + 1) * 512]
                            xl = xhat_l[:, c * 2048 + tg * 512:
                                        c * 2048 + (tg + 1) * 512]
                            e.matmul(pA[ab], wh, xh, start=(c == 0),
                                     stop=False)
                            e.matmul(pA[ab], wh, xl, start=False, stop=False)
                            e.matmul(pA[ab], wl, xh, start=False, stop=False)
                        lo = lb_half * 128
                        lh = lbs_h[:, lo:lo + 128]
                        ll = lbs_l[:, lo:lo + 128]
                        uh = u_h[:, tg * 512:(tg + 1) * 512]
                        ul = u_l[:, tg * 512:(tg + 1) * 512]
                        e.matmul(pA[ab], lh, uh, start=False, stop=False)
                        e.matmul(pA[ab], lh, ul, start=False, stop=False)
                        return e.matmul(pA[ab], ll, uh, start=False, stop=True)
                    mm_tok = P.mark_inst("tensor", qkv_mms)
                    last_mm = mm_tok
                    bias_col = bat[:, pi * 8 + g: pi * 8 + g + 1]
                    if pi < 2:
                        dh = qkv_names[pi][1][:, tg * 512:(tg + 1) * 512]
                        dl = qkv_names[pi][2][:, tg * 512:(tg + 1) * 512]
                        P.wait("scalar", mm_tok)

                        def hi_ev(e, dh=dh, ab=ab, bias_col=bias_col):
                            e.activation(out=dh, in_=pA[ab], func=AF.Identity,
                                         bias=bias_col)
                        P.s(hi_ev)
                        ms = P.mark("scalar")
                        P.wait("vector", ms)
                        P.v(lambda e, dl=dl, ab=ab, bias_col=bias_col, dh=dh:
                            e.scalar_tensor_tensor(
                                out=dl, in0=pA[ab], scalar=bias_col, in1=dh,
                                op0=ALU.add, op1=ALU.subtract))
                        ev_prev[ab] = P.mark("vector")
                    else:
                        P.wait("scalar", mm_tok)
                        P.s(lambda e, ab=ab, bias_col=bias_col: e.activation(
                            out=vstage, in_=pA[ab], func=AF.Identity,
                            bias=bias_col))
                        ms = P.mark("scalar")
                        ev_prev[ab] = ms
                        P.wait("tensor", ms)
                        ph = pT[tg % 2]
                        P.wait("tensor", ptev[tg % 2])
                        for j in range(4):
                            P.t(lambda e, ph=ph, j=j: e.transpose(
                                ph[:, j * 128:(j + 1) * 128],
                                vstage[:, j * 128:(j + 1) * 128], identt))
                        mt = P.mark("tensor")
                        P.wait("scalar", mt)
                        P.wait("vector", mt)
                        dsth = vt_h[:, tg * 512:(tg + 1) * 512]
                        dstl = vt_l[:, tg * 512:(tg + 1) * 512]
                        P.split(ph, dsth, dstl)
                        ptev[tg % 2] = P.mark("vector")
                ws_done(wh_half, last_mm)
                lb_done(lb_half, last_mm)
            tok_qkv = [P.mark("scalar"), P.mark("vector")]
            if s == 0 and g == 0:
                dbgt("qg_h", qg_h, tok_qkv)
                dbgt("qg_l", qg_l, tok_qkv)
                dbgt("kg_h", kg_h, tok_qkv)
                dbgt("vt_h", vt_h, tok_qkv)
                dbgt("vt_l", vt_l, tok_qkv)

            P.waits("tensor", tok_qkv)
            for h2 in range(2):
                hp = 64 * h2
                for qb in range(NTG):
                    nkt = 4 * (qb + 1)
                    npair = nkt // 2
                    yi = (h2 * NTG + qb) % 2
                    P.wait("tensor", ynorm_done[yi])

                    def emit_pv(jp, sp_tok, yi=yi, hp=hp, nkt=nkt):
                        P.wait("tensor", sp_tok)
                        tp = (0, hp) if hp else None

                        def pvmms(e, hp=hp, jp=jp, yi=yi, nkt=nkt, tp=tp):
                            last = None
                            for u in range(2):
                                j = 2 * jp + u
                                po = u * 512
                                vh = vt_h[:, j * 128 + hp: j * 128 + hp + 64]
                                vl = vt_l[:, j * 128 + hp: j * 128 + hp + 64]
                                ph_ = pt_h[:, po:po + 512]
                                pl_ = pt_l[:, po:po + 512]
                                st = (j == 0)
                                yps = pY[yi][hp:hp + 64, :]
                                sps = pA[yi][0:1, :]
                                e.matmul(yps, vh, ph_, start=st, stop=False,
                                         tile_position=tp)
                                e.matmul(yps, vh, pl_, start=False,
                                         stop=False, tile_position=tp)
                                e.matmul(yps, vl, ph_, start=False,
                                         stop=(j == nkt - 1),
                                         tile_position=tp)
                                e.matmul(sps, ones_bf, ph_, start=st,
                                         stop=False)
                                last = e.matmul(sps, ones_bf, pl_,
                                                start=False,
                                                stop=(j == nkt - 1))
                            return last
                        return P.mark_inst("tensor", pvmms)

                    for jp in range(npair):
                        kt0 = 2 * jp
                        P.wait("tensor", pair_exp)

                        def smms(e, hp=hp, kt0=kt0, qb=qb):
                            last = None
                            for u in range(2):
                                kt = kt0 + u
                                kh = kg_h[hp:hp + 64,
                                          kt * 128:(kt + 1) * 128]
                                kl = kg_l[hp:hp + 64,
                                          kt * 128:(kt + 1) * 128]
                                qh = qg_h[hp:hp + 64,
                                          qb * 512:(qb + 1) * 512]
                                ql = qg_l[hp:hp + 64,
                                          qb * 512:(qb + 1) * 512]
                                e.matmul(pS[u], kh, qh, start=True,
                                         stop=False)
                                e.matmul(pS[u], kh, ql, start=False,
                                         stop=False)
                                last = e.matmul(pS[u], kl, qh, start=False,
                                                stop=True)
                            return last
                        mm_tok = P.mark_inst("tensor", smms)
                        if jp >= 1:
                            pv_prev = emit_pv(jp - 1, sp_prev)
                        pre_exp = mm_tok
                        if kt0 >= 4 * qb:
                            P.wait("vector", mm_tok)
                            for u in range(2):
                                r = kt0 + u - 4 * qb
                                mc0 = 384 - 128 * r
                                P.v(lambda e, u=u, mc0=mc0: e.tensor_tensor(
                                    out=pS[u], in0=pS[u],
                                    in1=maskt[:, mc0:mc0 + 512], op=ALU.add))
                            pre_exp = P.mark("vector")
                        P.wait("scalar", pre_exp)
                        P.wait("scalar", sp_prev)
                        for u in range(2):
                            eb = ef32[u]
                            P.s(lambda e, u=u, eb=eb: e.activation(
                                out=eb, in_=pS[u], func=AF.Exp, scale=0.125,
                                bias=cv[:, CV_EXP:CV_EXP + 1]))
                        pair_exp = P.mark("scalar")
                        P.wait("vector", pair_exp)
                        P.wait("vector", pv_prev)
                        for u in range(2):
                            eb = ef32[u]
                            po = u * 512
                            P.v(lambda e, eb=eb, po=po: e.tensor_copy(
                                out=pt_h[:, po:po + 512], in_=eb))
                        for u in range(2):
                            eb = ef32[u]
                            po = u * 512
                            P.v(lambda e, eb=eb, po=po:
                                e.scalar_tensor_tensor(
                                    out=pt_l[:, po:po + 512], in0=eb,
                                    scalar=1.0, in1=pt_h[:, po:po + 512],
                                    op0=ALU.mult, op1=ALU.subtract))
                        sp_prev = P.mark("vector")
                    pv_prev = emit_pv(npair - 1, sp_prev)

                    P.wait("vector", pv_prev)
                    P.v(lambda e, yi=yi: e.reciprocal(
                        out=rrow, in_=pA[yi][0:1, :]))
                    mr = P.mark("vector")
                    P.wait("tensor", mr)
                    P.wait("tensor", ptev[0])
                    bc_tok = P.mark_inst("tensor", lambda e: e.matmul(
                        pT[0], ones_row, rrow, start=True, stop=True))
                    P.wait("scalar", bc_tok)
                    P.s(lambda e: e.activation(out=rb64, in_=pT[0],
                                               func=AF.Identity))
                    mbc = P.mark("scalar")
                    P.wait("vector", mbc)
                    P.v(lambda e, yi=yi, hp=hp: e.tensor_tensor(
                        out=ystage[hp:hp + 64, :], in0=pY[yi][hp:hp + 64, :],
                        in1=rb64[hp:hp + 64, :], op=ALU.mult))
                    yh = y_h[hp:hp + 64, g * 2048 + qb * 512:
                             g * 2048 + (qb + 1) * 512]
                    yl = y_l[hp:hp + 64, g * 2048 + qb * 512:
                             g * 2048 + (qb + 1) * 512]
                    P.v(lambda e, yh=yh, hp=hp: e.tensor_copy(
                        out=yh, in_=ystage[hp:hp + 64, :]))
                    P.v(lambda e, yh=yh, yl=yl, hp=hp: e.scalar_tensor_tensor(
                        out=yl, in0=ystage[hp:hp + 64, :], scalar=1.0, in1=yh,
                        op0=ALU.mult, op1=ALU.subtract))
                    ynorm_done[yi] = P.mark("vector")
                    ptev[0] = ynorm_done[yi]

        tok_y = P.mark("vector")
        if s == 0:
            dbgt("y_h", y_h[:, 0:4096], [tok_y])
            dbgt("y_l", y_l[:, 0:4096], [tok_y])
        if PHASES <= 3:
            break

        out_dma_toks = [None, None]
        pstage_guard = None
        xg = [None, None]
        P.wait("gpsimd", tok_y)
        for tg in range(NTG):
            P.waits("tensor", [tok_y])

            def u2_mms(e, tg=tg):
                last = None
                for c in range(8):
                    wh = lap_h[:, c * 32:(c + 1) * 32]
                    wl = lap_l[:, c * 32:(c + 1) * 32]
                    yh = y_h[:, c * 2048 + tg * 512: c * 2048 + (tg + 1) * 512]
                    yl = y_l[:, c * 2048 + tg * 512: c * 2048 + (tg + 1) * 512]
                    e.matmul(pA[0][0:32, :], wh, yh, start=(c == 0),
                             stop=False)
                    e.matmul(pA[0][0:32, :], wh, yl, start=False, stop=False)
                    last = e.matmul(pA[0][0:32, :], wl, yh, start=False,
                                    stop=(c == 7))
                return last
            mm_tok = P.mark_inst("tensor", u2_mms)
            P.wait("scalar", mm_tok)
            P.s(lambda e: e.activation(out=ustage, in_=pA[0][0:32, :],
                                       func=AF.Identity, scale=0.5))
            ms = P.mark("scalar")
            P.wait("vector", ms)
            P.v(lambda e: e.tensor_copy(out=u2_h, in_=ustage))
            P.v(lambda e: e.scalar_tensor_tensor(
                out=u2_l, in0=ustage, scalar=1.0, in1=u2_h, op0=ALU.mult,
                op1=ALU.subtract))
            mu2 = P.mark("vector")
            P.wait("tensor", mu2)

            pr_ev = [None, None]
            for oc in range(8):
                ab = oc % 2
                whalf, wtok = ws_load(
                    wslice(wproj_h, wproj_l, 0, 8, oc))
                P.waits("tensor", wtok)
                P.wait("tensor", pr_ev[ab])

                def proj_all(e, whalf=whalf, tg=tg, ab=ab, oc=oc):
                    off = whalf * 1024
                    for c in range(8):
                        wh = ws_h[:, off + c * 128: off + (c + 1) * 128]
                        wl = ws_l[:, off + c * 128: off + (c + 1) * 128]
                        yh = y_h[:, c * 2048 + tg * 512:
                                 c * 2048 + (tg + 1) * 512]
                        yl = y_l[:, c * 2048 + tg * 512:
                                 c * 2048 + (tg + 1) * 512]
                        e.matmul(pA[ab], wh, yh, start=(c == 0), stop=False)
                        e.matmul(pA[ab], wh, yl, start=False, stop=False)
                        e.matmul(pA[ab], wl, yh, start=False, stop=False)
                    lh = lbp_h[:, oc * 128:(oc + 1) * 128]
                    ll = lbp_l[:, oc * 128:(oc + 1) * 128]
                    e.matmul(pA[ab], lh, u2_h, start=False, stop=False)
                    e.matmul(pA[ab], lh, u2_l, start=False, stop=False)
                    return e.matmul(pA[ab], ll, u2_h, start=False, stop=True)
                mm_tok = P.mark_inst("tensor", proj_all)
                ws_done(whalf, mm_tok)
                P.wait("scalar", mm_tok)
                P.wait("scalar", pstage_guard)
                P.s(lambda e, ab=ab, oc=oc: e.activation(
                    out=pstage[:, oc * 512:(oc + 1) * 512], in_=pA[ab],
                    func=AF.Identity, bias=bpt[:, oc:oc + 1]))
                pr_ev[ab] = P.mark("scalar")
            tok_proj = P.mark("scalar")
            if s == 0 and tg == 0:
                dbgt("pstage", pstage, [tok_proj])
                dbgt("u2_h", u2_h, [tok_proj])

            P.waits("tensor", [tok_proj])
            for j in range(4):
                P.wait("gpsimd", xg[j % 2])
                tok_xr = P.dma("gpsimd", xres[:, (j % 2) * 1024:
                                               (j % 2) * 1024 + 1024],
                               x_r[s, tg * 4 + j])
                for half in range(2):
                    P.wait("tensor", ptev[half])
                    for jo in range(4):
                        oc = half * 4 + jo
                        P.t(lambda e, half=half, jo=jo, oc=oc, j=j:
                            e.transpose(pT[half][:, jo * 128:(jo + 1) * 128],
                                        pstage[:, oc * 512 + j * 128:
                                               oc * 512 + (j + 1) * 128],
                                        identt))
                mt = P.mark("tensor")
                P.wait("vector", mt)
                P.wait("vector", tok_xr)
                for half in range(2):
                    P.v(lambda e, half=half, j=j: e.tensor_tensor(
                        out=x1stg[:, half * 512:(half + 1) * 512],
                        in0=pT[half],
                        in1=xres[:, (j % 2) * 1024 + half * 512:
                                 (j % 2) * 1024 + (half + 1) * 512],
                        op=ALU.add))
                mv0 = P.mark("vector")
                P.wait("scalar", mv0)
                tix = (tg * 4 + j) * 1024
                P.split(x1stg, x1h[:, tix:tix + 1024], x1l[:, tix:tix + 1024])
                mvs = P.mark("vector")
                ptev[0] = mvs
                ptev[1] = mvs
                xg[j % 2] = mvs
            pstage_guard = P.mark("tensor")
        tok_x1all = [P.mark("scalar"), P.mark("vector")]
        if s == 0:
            dbgt("x1h", x1h[:, 0:4096], tok_x1all)
            dbgt("x1l", x1l[:, 0:4096], tok_x1all)

        ffn_done = None
        for tg in range(NTG):
            P.waits("scalar", tok_x1all)
            P.waits("vector", tok_x1all)
            for j in range(4):
                tix = (tg * 4 + j) * 1024
                if j == 0:
                    P.waits("vector", out_dma_toks)
                P.v(lambda e, tix=tix: e.tensor_tensor(
                    out=xfull, in0=x1h[:, tix:tix + 1024],
                    in1=x1l[:, tix:tix + 1024], op=ALU.add))
                mvx = P.mark("vector")
                P.wait("scalar", mvx)
                P.s(lambda e, j=j: e.activation(
                    out=sq_ln, in_=xfull, func=AF.Square,
                    accum_out=s2c[:, j:j + 1]))
                P.v(lambda e, j=j: e.reduce_sum(
                    out=s1c[:, j:j + 1], in_=xfull, axis=AX))
                msq = P.mark("scalar")
                P.wait("tensor", msq)
                beat()
                P.wait("vector", msq)
                P.drain("vector")
                P.v(lambda e, j=j: e.tensor_scalar(
                    out=mcol[:, j:j + 1], in0=s1c[:, j:j + 1],
                    scalar1=1.0 / D, scalar2=None, op0=ALU.mult))
                P.drain("vector")
                P.v(lambda e, j=j: e.tensor_tensor(
                    out=vcol0[:, j:j + 1], in0=mcol[:, j:j + 1],
                    in1=mcol[:, j:j + 1], op=ALU.mult))
                P.drain("vector")
                P.v(lambda e, j=j: e.scalar_tensor_tensor(
                    out=vcol[:, j:j + 1], in0=s2c[:, j:j + 1],
                    scalar=1.0 / D, in1=vcol0[:, j:j + 1], op0=ALU.mult,
                    op1=ALU.subtract))
                mvc = P.mark("vector")
                P.wait("tensor", mvc)
                beat()
                P.wait("scalar", mvc)
                P.s(lambda e, j=j: e.activation(
                    out=rstd0[:, j:j + 1], in_=vcol[:, j:j + 1], func=AF.Sqrt,
                    bias=cv[:, CV_EPS:CV_EPS + 1]))
                msc2 = P.mark("scalar")
                P.wait("vector", msc2)
                P.v(lambda e, j=j: e.reciprocal(out=rstd[:, j:j + 1],
                                                in_=rstd0[:, j:j + 1]))
                P.drain("vector")
                P.v(lambda e, j=j: e.scalar_tensor_tensor(
                    out=mbias[:, j:j + 1], in0=mcol[:, j:j + 1], scalar=-1.0,
                    in1=rstd[:, j:j + 1], op0=ALU.mult, op1=ALU.mult))
                mvc2 = P.mark("vector")
                P.wait("tensor", mvc2)
                beat()
                P.wait("scalar", mvc2)
                P.s(lambda e, j=j: e.activation(
                    out=xh_tm, in_=xfull, func=AF.Identity,
                    scale=rstd[:, j:j + 1], bias=mbias[:, j:j + 1]))
                msx = P.mark("scalar")
                P.wait("tensor", msx)
                for half in range(2):
                    P.wait("tensor", ptev[half])
                    for jo in range(4):
                        c = half * 4 + jo
                        P.t(lambda e, half=half, jo=jo, c=c: e.transpose(
                            pT[half][:, jo * 128:(jo + 1) * 128],
                            xh_tm[:, c * 128:(c + 1) * 128], identt))
                mt = P.mark("tensor")
                P.wait("scalar", mt)
                P.wait("vector", mt)
                for half in range(2):
                    c0 = half * 4
                    dsth = xh1_h.rearrange("p (c w) -> p c w", c=8)[
                        :, c0:c0 + 4, j * 128:(j + 1) * 128]
                    dstl = xh1_l.rearrange("p (c w) -> p c w", c=8)[
                        :, c0:c0 + 4, j * 128:(j + 1) * 128]
                    ptr = pT[half].rearrange("p (c w) -> p c w", c=4)
                    P.split(ptr, dsth, dstl)
                    ptev[half] = P.mark("vector")
            tok_xh1 = [P.mark("scalar"), P.mark("vector")]

            P.waits("tensor", tok_xh1)
            gel_ev = None
            z_done = [None, None]
            hbank = [pA[0], pA[1], pS[0], pS[1]]
            SETS = [GB, GB2]
            ZS = [GB[6], GB[7]]
            TS = [GB2[6], GB2[7]]
            for fp in range(16):
                f0 = 2 * fp
                bp = fp % 2
                if fp >= 1:
                    P.wait("tensor", mv)
                    beat()
                    P.wait("tensor", msg)
                    beat()
                    P.wait("tensor", gel_ev)
                    beat()
                P.wait("tensor", z_done[bp])
                mm_tok = None
                for u in range(2):
                    f = f0 + u
                    whalf, wtok = ws_load(wslice(w1_hs, w1_ls, 0, 8, f))
                    P.waits("tensor", wtok)
                    pb = hbank[bp * 2 + u]

                    def h_mms(e, whalf=whalf, pb=pb):
                        off = whalf * 1024
                        last = None
                        for c in range(8):
                            wh = ws_h[:, off + c * 128: off + (c + 1) * 128]
                            wl = ws_l[:, off + c * 128: off + (c + 1) * 128]
                            xh = xh1_h[:, c * 512:(c + 1) * 512]
                            xl = xh1_l[:, c * 512:(c + 1) * 512]
                            e.matmul(pb, wh, xh, start=(c == 0), stop=False)
                            e.matmul(pb, wh, xl, start=False, stop=False)
                            last = e.matmul(pb, wl, xh, start=False,
                                            stop=(c == 7))
                        return last
                    mm_tok = P.mark_inst("tensor", h_mms)
                    ws_done(whalf, mm_tok)
                P.wait("scalar", mm_tok)
                P.wait("scalar", gel_ev)
                for u in range(2):
                    pb = hbank[bp * 2 + u]
                    P.s(lambda e, pb=pb, u=u, f0=f0: e.activation(
                        out=ZS[u], in_=pb, func=AF.Identity,
                        scale=256.0, bias=b1t[:, f0 + u:f0 + u + 1]))
                if fp >= 1:
                    for u in range(2):
                        fprev = f0 - 2 + u
                        ghp = hgel_h[:, fprev * 512:(fprev + 1) * 512]
                        P.s(lambda e, ghp=ghp, W=SETS[u][2]: e.activation(
                            out=ghp, in_=W, func=AF.Identity,
                            scale=1.0 / 256.0))
                z_tok = P.mark("scalar")
                z_done[bp] = z_tok
                P.wait("vector", z_tok)
                if fp == 0:
                    P.waits("vector", out_dma_toks)
                if fp >= 1:
                    for u in range(2):
                        fprev = f0 - 2 + u
                        ghp = hgel_h[:, fprev * 512:(fprev + 1) * 512]
                        glp = hgel_l[:, fprev * 512:(fprev + 1) * 512]
                        P.v(lambda e, ghp=ghp, glp=glp, W=SETS[u][2]:
                            e.scalar_tensor_tensor(
                                out=glp, in0=W, scalar=1.0 / 256.0,
                                in1=ghp, op0=ALU.mult, op1=ALU.subtract))
                for u in range(2):
                    S = SETS[u]
                    P.v(lambda e, S=S, zb=ZS[u]: e.tensor_scalar(
                        out=S[0], in0=zb, scalar1=-1024.0, scalar2=1024.0,
                        op0=ALU.max, op1=ALU.min))
                    P.rnd(S[3], S[0], S[1], S[2])
                    P.v(lambda e, S=S: e.tensor_scalar(
                        out=S[1], in0=S[0], scalar1=31.5,
                        scalar2=1.0 / 64.0, op0=ALU.subtract, op1=ALU.mult))
                    P.rnd(S[4], S[1], S[2], S[0])
                    P.v(lambda e, S=S: e.scalar_tensor_tensor(
                        out=S[2], in0=S[4], scalar=-64.0, in1=S[3],
                        op0=ALU.mult, op1=ALU.add))
                mv = P.mark("vector")
                P.wait("scalar", mv)
                for u in range(2):
                    S = SETS[u]
                    P.s(lambda e, S=S: e.activation(
                        out=S[1], in_=S[4], func=AF.Gelu, scale=0.25))
                    P.s(lambda e, S=S: e.activation(
                        out=S[0], in_=S[4], func=AF.Gelu, scale=0.25,
                        bias=cv[:, CV_Y1:CV_Y1 + 1]))
                    P.s(lambda e, zb=ZS[u], u=u: e.activation(
                        out=TS[u], in_=zb, func=AF.Relu,
                        bias=cv[:, CV_NT:CV_NT + 1]))
                msg = P.mark("scalar")
                P.wait("vector", msg)
                for u in range(2):
                    S = SETS[u]
                    P.v(lambda e, S=S: e.tensor_tensor(
                        out=S[5], in0=S[0], in1=S[1], op=ALU.subtract))
                    P.v(lambda e, S=S: e.tensor_tensor(
                        out=S[3], in0=S[2], in1=S[5], op=ALU.mult))
                    P.v(lambda e, S=S: e.scalar_tensor_tensor(
                        out=S[2], in0=S[3], scalar=1.0 / 64.0, in1=S[1],
                        op0=ALU.mult, op1=ALU.add))
                    P.v(lambda e, S=S, u=u: e.scalar_tensor_tensor(
                        out=S[1], in0=S[2], scalar=256.0, in1=TS[u],
                        op0=ALU.mult, op1=ALU.add))
                    P.rnd(S[2], S[1], S[0], S[3])
                gel_ev = P.mark("vector")
            P.wait("scalar", gel_ev)
            for u in range(2):
                fprev = 30 + u
                ghp = hgel_h[:, fprev * 512:(fprev + 1) * 512]
                P.s(lambda e, ghp=ghp, W=SETS[u][2]: e.activation(
                    out=ghp, in_=W, func=AF.Identity, scale=1.0 / 256.0))
            fl_tok = P.mark("scalar")
            P.wait("vector", fl_tok)
            for u in range(2):
                fprev = 30 + u
                ghp = hgel_h[:, fprev * 512:(fprev + 1) * 512]
                glp = hgel_l[:, fprev * 512:(fprev + 1) * 512]
                P.v(lambda e, ghp=ghp, glp=glp, W=SETS[u][2]:
                    e.scalar_tensor_tensor(
                        out=glp, in0=W, scalar=1.0 / 256.0, in1=ghp,
                        op0=ALU.mult, op1=ALU.subtract))
            tok_hgel = P.mark("vector")
            if s == 0 and tg == 0:
                dbgt("hgel_h", hgel_h[:, 0:2048], [tok_hgel])
                dbgt("xh1_h", xh1_h[:, 0:1024], [tok_hgel])

            P.waits("tensor", [tok_hgel])
            oq_ev = [None, None]
            o_last = None
            for oc in range(8):
                ab = oc % 2
                P.wait("tensor", oq_ev[ab])
                mmq = None
                for q in range(4):
                    hq, wtokq = ws_load(
                        wslice(w2_hs, w2_ls, q * 8, (q + 1) * 8, oc))
                    P.waits("tensor", wtokq)

                    def o_mms(e, hq=hq, ab=ab, q=q):
                        off = hq * 1024
                        last = None
                        for ci in range(8):
                            c = q * 8 + ci
                            wh = ws_h[:, off + ci * 128: off + (ci + 1) * 128]
                            wl = ws_l[:, off + ci * 128: off + (ci + 1) * 128]
                            hh = hgel_h[:, c * 512:(c + 1) * 512]
                            hl = hgel_l[:, c * 512:(c + 1) * 512]
                            e.matmul(pA[ab], wh, hh, start=(c == 0),
                                     stop=False)
                            e.matmul(pA[ab], wh, hl, start=False, stop=False)
                            last = e.matmul(pA[ab], wl, hh, start=False,
                                            stop=(c == 31))
                        return last
                    mmq = P.mark_inst("tensor", o_mms)
                    ws_done(hq, mmq)
                o_last = mmq
                P.wait("scalar", mmq)
                P.s(lambda e, ab=ab, oc=oc: e.activation(
                    out=GB[4], in_=pA[ab], func=AF.Identity, scale=256.0,
                    bias=b2t[:, oc:oc + 1]))
                mso = P.mark("scalar")
                P.wait("vector", mso)
                P.rnd(GB[5], GB[4], GB[6], GB[7])
                P.v(lambda e, oc=oc: e.tensor_scalar(
                    out=ofm[:, oc * 512:(oc + 1) * 512], in0=GB[5],
                    scalar1=1.0 / 256.0, scalar2=None, op0=ALU.mult))
                oq_ev[ab] = P.mark("vector")
            ffn_done = o_last
            tok_ofm = P.mark("vector")

            P.waits("tensor", [tok_ofm])
            for j in range(4):
                for half in range(2):
                    P.wait("tensor", ptev[half])
                    for jo in range(4):
                        oc = half * 4 + jo
                        P.t(lambda e, half=half, jo=jo, oc=oc, j=j:
                            e.transpose(pT[half][:, jo * 128:(jo + 1) * 128],
                                        ofm[:, oc * 512 + j * 128:
                                            oc * 512 + (j + 1) * 128],
                                        identt))
                mt = P.mark("tensor")
                P.wait("vector", mt)
                osl = (j % 2) * 1024
                P.wait("vector", out_dma_toks[j % 2])
                tix = (tg * 4 + j) * 1024
                for half in range(2):
                    P.v(lambda e, half=half, tix=tix, osl=osl:
                        e.scalar_tensor_tensor(
                            out=ot_st[:, osl + half * 512:
                                      osl + (half + 1) * 512],
                            in0=x1h[:, tix + half * 512:
                                    tix + (half + 1) * 512],
                            scalar=1.0, in1=pT[half], op0=ALU.mult,
                            op1=ALU.add))
                    P.v(lambda e, half=half, tix=tix, osl=osl:
                        e.tensor_tensor(
                            out=ot_st[:, osl + half * 512:
                                      osl + (half + 1) * 512],
                            in0=ot_st[:, osl + half * 512:
                                      osl + (half + 1) * 512],
                            in1=x1l[:, tix + half * 512:
                                    tix + (half + 1) * 512],
                            op=ALU.add))
                mv = P.mark("vector")
                ptev[0] = mv
                ptev[1] = mv
                P.wait("sync", mv)
                out_dma_toks[j % 2] = P.dma("sync", out_r[s, tg * 4 + j],
                                            ot_st[:, osl:osl + 1024])

        tok_seq_end = P.mark("tensor")
        tok_out_prev = out_dma_toks[1]

    P.wait("sync", ("d_sync", P.cnt["d_sync"]))
    P.wait("gpsimd", ("d_gpsimd", P.cnt["d_gpsimd"]))

    with stack:
        with nc.Block() as block:
            P.emit(block)

    return nc


def host_inputs(L0_attn_x, L0_attn_w_attn, L0_attn_b_attn, L0_attn_la_attn,
                L0_attn_lb_attn, L0_attn_w_proj, L0_attn_b_proj,
                L0_attn_la_proj, L0_attn_lb_proj, L0_ffn_w1, L0_ffn_b1,
                L0_ffn_w2, L0_ffn_b2):
    C = np.ascontiguousarray
    asn = np.asarray
    wqkvT = C(asn(L0_attn_w_attn).T)
    battn = C(asn(L0_attn_b_attn).reshape(24, 128).T)
    laTa = C(asn(L0_attn_la_attn).T.reshape(8, 128, 32).transpose(1, 0, 2)
             .reshape(128, 256))
    lbTa = C(asn(L0_attn_lb_attn).T)
    wprojT = C(asn(L0_attn_w_proj).T)
    bproj = C(asn(L0_attn_b_proj).reshape(8, 128).T)
    laTp = C(asn(L0_attn_la_proj).T.reshape(8, 128, 32).transpose(1, 0, 2)
             .reshape(128, 256))
    lbTp = C(asn(L0_attn_lb_proj).T)
    w1 = C(asn(L0_ffn_w1))
    b1c = C(asn(L0_ffn_b1).reshape(32, 128).T)
    w2 = C(asn(L0_ffn_w2))
    b2c = C(asn(L0_ffn_b2).reshape(8, 128).T)

    identx = np.eye(128, dtype=np.float32)
    u = np.arange(896)[None, :]
    p = np.arange(128)[:, None]
    import ml_dtypes
    maskx = np.where(u >= p + 384, 0.0, -1e30).astype(ml_dtypes.bfloat16)
    cvec = np.zeros((128, 8), np.float32)
    cvec[:, 0] = 0.25
    cvec[:, 1] = -12.0
    cvec[:, 2] = 1e-5
    cvec[:, 3] = -1024.0

    x = asn(L0_attn_x)
    shared = dict(wqkvT=wqkvT, battn=battn, laTa=laTa, lbTa=lbTa,
                  wprojT=wprojT, bproj=bproj, laTp=laTp, lbTp=lbTp,
                  w1=w1, b1c=b1c, w2=w2, b2c=b2c, identx=identx,
                  maskx=maskx, cvec=cvec)
    in_maps = []
    for core in range(8):
        d = dict(shared)
        d["x"] = C(x[2 * core: 2 * core + 2])
        in_maps.append(d)
    return in_maps


_NC_CACHE = {}


def kernel(**inputs):
    in_maps = host_inputs(**inputs)
    if "nc" not in _NC_CACHE:
        _NC_CACHE["nc"] = build()
    nc = _NC_CACHE["nc"]
    res = run_bass_kernel_spmd(nc, in_maps, core_ids=list(range(8)))
    out = np.concatenate([np.asarray(r["out"]) for r in res.results], axis=0)
    return out.astype(np.float32)


if __name__ == "__main__":
    nc = build()
    n = sum(1 for _ in nc.all_instructions())
    print("instructions:", n)


# version 15
# speedup vs baseline: 1.1095x; 1.1095x over previous
import contextlib
import os
import numpy as np
import concourse.bass as bass
import concourse.mybir as mybir
from concourse.bass_utils import run_bass_kernel_spmd

F32 = mybir.dt.float32
BF16 = mybir.dt.bfloat16
AF = mybir.ActivationFunctionType
ALU = mybir.AluOpType
AX = mybir.AxisListType.X

C23 = float(3 * 2.0**22)
C52 = float(3 * 2.0**51)
C36 = float(3 * 2.0**35)
C37 = float(3 * 2.0**36)

T = 2048
D = 1024
NSEQ = 2
NTILE = 16
NTG = 4
NHG = 8
FF = 4096

ENGINES = ("sync", "gpsimd", "tensor", "vector", "scalar")


class Prog:
    def __init__(self, nc):
        self.nc = nc
        self.ops = {e: [] for e in ENGINES}
        self.cnt = {}
        self.seen = {e: {} for e in ENGINES}
        self.sems = {}

    def make_sems(self, stack):
        for n in ["d_sync", "d_gpsimd", "m_tensor", "m_vector", "m_scalar",
                  "m_gpsimd"]:
            self.sems[n] = stack.enter_context(self.nc.semaphore(n))
            self.cnt[n] = 0

    def op(self, eng, fn):
        self.ops[eng].append(fn)

    def dma(self, q, out, in_):
        s = "d_" + q
        self.cnt[s] += 16
        v = self.cnt[s]
        sem = self.sems[s]

        def f(e, out=out, in_=in_, sem=sem):
            e.dma_start(out=out, in_=in_).then_inc(sem, 16)
        self.op(q, f)
        return (s, v)

    def wait(self, eng, tok):
        if tok is None:
            return
        s, v = tok
        if self.seen[eng].get(s, 0) >= v:
            return
        self.seen[eng][s] = v
        sem = self.sems[s]
        self.op(eng, lambda e, sem=sem, v=v: e.wait_ge(sem, v))

    def waits(self, eng, toks):
        for t in toks:
            self.wait(eng, t)

    def mark(self, eng):
        s = "m_" + eng
        self.cnt[s] += 1
        v = self.cnt[s]
        sem = self.sems[s]
        self.op(eng, lambda e, sem=sem: e.maybe_drain_then_inc((sem, 1)))
        return (s, v)

    def mark_inst(self, eng, fn):
        s = "m_" + eng
        self.cnt[s] += 1
        v = self.cnt[s]
        sem = self.sems[s]

        def f(e, fn=fn, sem=sem):
            fn(e).then_inc(sem, 1)
        self.op(eng, f)
        return (s, v)

    def t(self, fn):
        self.op("tensor", fn)

    def v(self, fn):
        self.op("vector", fn)

    def s(self, fn):
        self.op("scalar", fn)

    def g(self, fn):
        self.op("gpsimd", fn)

    def drain(self, eng):
        self.op(eng, lambda e: e.drain())

    def rnd(self, out, in_, t1, t2, cbig=None):
        nflag = self.nflag_col

        def f(e, out=out, in_=in_, t1=t1, t2=t2, nflag=nflag):
            e.tensor_scalar(out=t1, in0=in_, scalar1=C23, scalar2=C23,
                            op0=ALU.add, op1=ALU.subtract)
            e.tensor_scalar(out=t2, in0=in_, scalar1=C52, scalar2=C52,
                            op0=ALU.add, op1=ALU.subtract)
            e.scalar_tensor_tensor(out=t2, in0=in_, scalar=nflag, in1=t2,
                                   op0=ALU.mult, op1=ALU.add)
            e.tensor_tensor(out=out, in0=t1, in1=t2, op=ALU.add)
        self.op("vector", f)

    def split(self, src, hi, lo, bias=None):
        if bias is None:
            self.s(lambda e, src=src, hi=hi: e.activation(
                out=hi, in_=src, func=AF.Identity))
        else:
            self.s(lambda e, src=src, hi=hi, bias=bias: e.activation(
                out=hi, in_=src, func=AF.Identity, bias=bias))
        ms = self.mark("scalar")
        self.wait("vector", ms)
        if bias is None:
            self.v(lambda e, src=src, hi=hi, lo=lo: e.scalar_tensor_tensor(
                out=lo, in0=src, scalar=1.0, in1=hi,
                op0=ALU.mult, op1=ALU.subtract))
        else:
            self.v(lambda e, src=src, hi=hi, lo=lo, bias=bias:
                   e.scalar_tensor_tensor(out=lo, in0=src, scalar=bias,
                                          in1=hi, op0=ALU.add,
                                          op1=ALU.subtract))

    def emit(self, block):
        for eng in ENGINES:
            ops = self.ops[eng]
            deco = getattr(block, eng)

            def body(e, ops=ops):
                for f in ops:
                    f(e)
            deco(body)


def build():
    nc = bass.Bass("TRN2", target_bir_lowering=False,
                   detect_race_conditions=False)
    stack = contextlib.ExitStack()
    ap = lambda h: h.ap()

    x_in = ap(nc.dram_tensor("x", [NSEQ, T, D], F32, kind="ExternalInput"))
    wqkvT = ap(nc.dram_tensor("wqkvT", [D, 3 * D], F32, kind="ExternalInput"))
    battn = ap(nc.dram_tensor("battn", [128, 24], F32, kind="ExternalInput"))
    laTa = ap(nc.dram_tensor("laTa", [128, 256], F32, kind="ExternalInput"))
    lbTa = ap(nc.dram_tensor("lbTa", [32, 3 * D], F32, kind="ExternalInput"))
    wprojT = ap(nc.dram_tensor("wprojT", [D, D], F32, kind="ExternalInput"))
    bproj = ap(nc.dram_tensor("bproj", [128, 8], F32, kind="ExternalInput"))
    laTp = ap(nc.dram_tensor("laTp", [128, 256], F32, kind="ExternalInput"))
    lbTp = ap(nc.dram_tensor("lbTp", [32, D], F32, kind="ExternalInput"))
    w1 = ap(nc.dram_tensor("w1", [D, FF], F32, kind="ExternalInput"))
    b1c = ap(nc.dram_tensor("b1c", [128, 32], F32, kind="ExternalInput"))
    w2 = ap(nc.dram_tensor("w2", [FF, D], F32, kind="ExternalInput"))
    b2c = ap(nc.dram_tensor("b2c", [128, 8], F32, kind="ExternalInput"))
    identx = ap(nc.dram_tensor("identx", [128, 128], F32, kind="ExternalInput"))
    maskx = ap(nc.dram_tensor("maskx", [128, 896], BF16, kind="ExternalInput"))
    cvec = ap(nc.dram_tensor("cvec", [128, 8], F32, kind="ExternalInput"))
    out_hbm = ap(nc.dram_tensor("out", [NSEQ, T, D], F32,
                                kind="ExternalOutput"))

    wqkv_h = ap(nc.dram_tensor("wqkv_h", [24, 8, 128, 128], BF16))
    wqkv_l = ap(nc.dram_tensor("wqkv_l", [24, 8, 128, 128], BF16))
    wproj_h = ap(nc.dram_tensor("wproj_h", [8, 8, 128, 128], BF16))
    wproj_l = ap(nc.dram_tensor("wproj_l", [8, 8, 128, 128], BF16))
    w1_hs = ap(nc.dram_tensor("w1_hs", [32, 8, 128, 128], BF16))
    w1_ls = ap(nc.dram_tensor("w1_ls", [32, 8, 128, 128], BF16))
    w2_hs = ap(nc.dram_tensor("w2_hs", [8, 32, 128, 128], BF16))
    w2_ls = ap(nc.dram_tensor("w2_ls", [8, 32, 128, 128], BF16))

    sbuf = lambda nm, sh, dt=F32: nc.alloc_sbuf_tensor(nm, sh, dt).ap()
    B1 = sbuf("B1", [128, 32768], BF16)
    B2 = sbuf("B2", [128, 32768], BF16)
    B3 = sbuf("B3", [128, 14336], BF16)
    F1 = sbuf("F1", [128, 8192], F32)

    xhat_h = B1[:, 0:16384]
    xhat_l = B1[:, 16384:32768]
    x1h = B1[:, 0:16384]
    x1l = B1[:, 16384:32768]

    y_h = B2[:, 0:16384]
    y_l = B2[:, 16384:32768]
    hgel_h = B2[:, 0:16384]
    hgel_l = B2[:, 16384:32768]
    wsp_h = B2[:, 0:2048]
    wsp_l = B2[:, 2048:4096]
    u_l = B2[0:32, 14336:16384]
    u_h = B2[0:32, 30720:32768]

    qg_h = B3[:, 0:2048]
    qg_l = B3[:, 2048:4096]
    kg_h = B3[:, 4096:6144]
    kg_l = B3[:, 6144:8192]
    vt_h = B3[:, 8192:10240]
    vt_l = B3[:, 10240:12288]
    pt_h = B3[:, 12288:13312]
    pt_l = B3[:, 13312:14336]
    xh1_h = B3[:, 0:4096]
    xh1_l = B3[:, 4096:8192]
    u2_h = B3[0:32, 12288:12800]
    u2_l = B3[0:32, 12800:13312]

    wsplit = F1[:, 0:2048]
    xin = F1[:, 0:2048]
    xh_tm = F1[:, 2048:3072]
    sq_scr1 = F1[:, 3072:4096]
    sq_ln = sq_scr1
    vstage = F1[:, 4096:4608]
    ef32 = [F1[:, 4608 + i * 512: 4608 + (i + 1) * 512] for i in range(2)]
    rb64 = F1[:, 5632:6144]
    ystage = F1[:, 6144:6656]
    rrow = F1[0:1, 6656:7168]
    pstage = F1[:, 0:4096]
    x1stg = F1[:, 4096:5120]
    ustage = F1[0:32, 5120:5632]
    xres = F1[:, 5632:7680]
    GB = [F1[:, i * 512: (i + 1) * 512] for i in range(8)]
    GB2 = [F1[:, 4096 + i * 512: 4096 + (i + 1) * 512]
           for i in range(8)]
    ot_st = F1[:, 0:2048]
    xfull = F1[:, 0:1024]
    ofm = F1[:, 4096:8192]

    ws_h = sbuf("ws_h", [128, 2048], BF16)
    ws_l = sbuf("ws_l", [128, 2048], BF16)

    identt = sbuf("identt", [128, 128])
    maskt = sbuf("maskt", [128, 896], BF16)
    cv = sbuf("cv", [128, 8])
    bat = sbuf("bat", [128, 24])
    bpt = sbuf("bpt", [128, 8])
    b1t = sbuf("b1t", [128, 32])
    b2t = sbuf("b2t", [128, 8])
    la_h = sbuf("la_h", [128, 256], BF16)
    la_l = sbuf("la_l", [128, 256], BF16)
    lap_h = sbuf("lap_h", [128, 256], BF16)
    lap_l = sbuf("lap_l", [128, 256], BF16)
    lbp_h = sbuf("lbp_h", [32, D], BF16)
    lbp_l = sbuf("lbp_l", [32, D], BF16)
    lbs_h = sbuf("lbs_h", [32, 256], BF16)
    lbs_l = sbuf("lbs_l", [32, 256], BF16)
    lbstg = sbuf("lbstg", [32, 256])
    s1c = sbuf("s1c", [128, 16])
    s2c = sbuf("s2c", [128, 16])
    mcol = sbuf("mcol", [128, 16])
    vcol = sbuf("vcol", [128, 16])
    rstd = sbuf("rstd", [128, 16])
    rstd0 = sbuf("rstd0", [128, 16])
    vcol0 = sbuf("vcol0", [128, 16])
    mbias = sbuf("mbias", [128, 16])
    ones_bf = sbuf("ones_bf", [128, 1], BF16)
    ones_row = sbuf("ones_row", [1, 128], F32)
    fcol = sbuf("fcol", [128, 4], F32)

    pball = lambda nm: nc.alloc_psum_tensor(nm, [128, 512], F32).ap()
    pA = [pball("pA0"), pball("pA1")]
    pT = [pball("pT0"), pball("pT1")]
    pS = [pball("pS0"), pball("pS1")]
    pY = [pball("pY0"), pball("pY1")]

    P = Prog(nc)
    P.make_sems(stack)
    CV_Y1, CV_EXP, CV_EPS, CV_NT = 0, 1, 2, 3
    DBG = bool(os.environ.get("KDBG"))
    PHASES = int(os.environ.get("KPHASE", "99"))

    def dbgt(name, src_ap, toks):
        if not DBG:
            return
        t = nc.dram_tensor("dbg_" + name, list(src_ap.shape), src_ap.dtype,
                           kind="ExternalOutput").ap()
        P.waits("sync", list(toks))
        P.dma("sync", t, src_ap)

    x_r = x_in.rearrange("s (n p) d -> s n p d", p=128)
    out_r = out_hbm.rearrange("s (n p) d -> s n p d", p=128)

    tok_consts = [P.dma("sync", identt, identx), P.dma("sync", maskt, maskx),
                  P.dma("sync", cv, cvec), P.dma("sync", bat, battn),
                  P.dma("sync", bpt, bproj)]
    tok_b = [P.dma("sync", b1t, b1c), P.dma("sync", b2t, b2c)]
    P.waits("vector", tok_b)
    P.v(lambda e: e.tensor_scalar(out=b1t, in0=b1t, scalar1=256.0,
                                  scalar2=None, op0=ALU.mult))
    P.v(lambda e: e.tensor_scalar(out=b2t, in0=b2t, scalar1=256.0,
                                  scalar2=None, op0=ALU.mult))
    P.v(lambda e: e.memset(ones_bf, 1.0))
    P.v(lambda e: e.memset(ones_row, 1.0))
    P.v(lambda e: e.memset(fcol[:, 0:1], 1.0))
    P.drain("vector")
    P.v(lambda e: e.tensor_scalar(out=fcol[:, 1:2], in0=fcol[:, 0:1],
                                  scalar1=float(2.0**-40), scalar2=1.0,
                                  op0=ALU.add, op1=ALU.subtract))
    P.drain("vector")
    P.v(lambda e: e.tensor_scalar(out=fcol[:, 2:3], in0=fcol[:, 1:2],
                                  scalar1=float(2.0**40), scalar2=None,
                                  op0=ALU.mult))
    P.drain("vector")
    P.v(lambda e: e.tensor_scalar(out=fcol[:, 3:4], in0=fcol[:, 2:3],
                                  scalar1=-1.0, scalar2=None, op0=ALU.mult))
    P.drain("vector")
    P.flag_col = fcol[:, 2:3]
    P.nflag_col = fcol[:, 3:4]

    prev_split = [None]

    def beat():
        P.t(lambda e: e.matmul(pY[0][0:1, 0:1], ones_bf, ones_bf,
                               start=True, stop=True))

    def split_to_hbm(src, hi, lo, rows, cols):
        for r0 in range(0, rows, 128):
            ci = r0 // 128
            c = 0
            while c < cols:
                w = min(2048, cols - c)
                nb = w // 128
                o0 = c // 128
                t_in = P.dma("sync", wsplit[:, 0:w], src[r0:r0 + 128, c:c + w])
                P.waits("vector", [t_in, prev_split[0]])
                P.waits("scalar", [t_in, prev_split[0]])
                P.split(wsplit[:, 0:w], wsp_h[:, 0:w], wsp_l[:, 0:w])
                mv = P.mark("vector")
                P.wait("tensor", t_in)
                beat()
                P.wait("tensor", mv)
                beat()
                P.wait("sync", mv)
                srch = wsp_h[:, 0:w].rearrange("p (o x) -> p o x", o=nb)
                srcl = wsp_l[:, 0:w].rearrange("p (o x) -> p o x", o=nb)
                dsth = hi[o0:o0 + nb, ci].rearrange("o p x -> p o x")
                dstl = lo[o0:o0 + nb, ci].rearrange("o p x -> p o x")
                P.dma("sync", dsth, srch)
                prev_split[0] = P.dma("sync", dstl, srcl)
                c += w

    split_to_hbm(wqkvT, wqkv_h, wqkv_l, D, 3 * D)
    split_to_hbm(wprojT, wproj_h, wproj_l, D, D)
    split_to_hbm(w1, w1_hs, w1_ls, D, FF)
    split_to_hbm(w2, w2_hs, w2_ls, FF, D)

    def load_split_small(src, hi, lo, prow, cols):
        t_in = P.dma("sync", wsplit[0:prow, 0:cols], src)
        P.waits("scalar", [t_in, prev_split[0]])
        P.waits("vector", [t_in, prev_split[0]])
        P.split(wsplit[0:prow, 0:cols], hi, lo)
        mv = P.mark("vector")
        P.wait("sync", mv)
        prev_split[0] = mv
        return mv

    load_split_small(laTa, la_h, la_l, 128, 256)
    load_split_small(laTp, lap_h, lap_l, 128, 256)
    tok_small = load_split_small(lbTp, lbp_h, lbp_l, 32, D)
    P.waits("tensor", tok_consts + [tok_small])
    P.waits("scalar", tok_consts)
    P.waits("vector", tok_consts)
    P.waits("gpsimd", tok_consts)

    ws_state = {"idx": 0, "guard": [None, None]}

    def ws_load(srcs):
        h = ws_state["idx"] % 2
        ws_state["idx"] += 1
        P.wait("sync", ws_state["guard"][h])
        P.wait("gpsimd", ws_state["guard"][h])
        off = h * 1024
        tks = []
        pos = 0
        for cols, sh, sl, nch in srcs:
            dsth = ws_h[:, off + pos: off + pos + cols].rearrange(
                "p (c o) -> p c o", c=nch)
            dstl = ws_l[:, off + pos: off + pos + cols].rearrange(
                "p (c o) -> p c o", c=nch)
            tks.append(P.dma("sync", dsth, sh.rearrange("c p o -> p c o")))
            tks.append(P.dma("sync", dstl, sl.rearrange("c p o -> p c o")))
            pos += cols
        return h, tks

    def ws_done(h, tok):
        ws_state["guard"][h] = tok

    lb_state = {"idx": 0, "guard": [None, None]}

    def lb_load(col):
        h = lb_state["idx"] % 2
        lb_state["idx"] += 1
        P.wait("sync", lb_state["guard"][h])
        tk = P.dma("sync", lbstg[:, h * 128:(h + 1) * 128],
                   lbTa[:, col:col + 128])
        P.waits("scalar", [tk])
        P.waits("vector", [tk])
        P.split(lbstg[:, h * 128:(h + 1) * 128],
                lbs_h[:, h * 128:(h + 1) * 128],
                lbs_l[:, h * 128:(h + 1) * 128])
        return h, P.mark("vector")

    def lb_done(h, tok):
        lb_state["guard"][h] = tok

    def wslice(hr, lr, c0, c1, oblk):
        return [((c1 - c0) * 128, hr[oblk, c0:c1], lr[oblk, c0:c1], c1 - c0)]

    tok_seq_end = None
    tok_out_prev = None

    tok_p0_out = prev_split[0]
    for s in range(NSEQ):
        P.wait("scalar", tok_p0_out)
        P.wait("vector", tok_p0_out)
        P.wait("gpsimd", tok_p0_out)
        norm_done = [None, None]
        ptev = [None, None]
        for t in range(NTILE):
            sl = (t % 2) * 1024
            P.wait("gpsimd", norm_done[t % 2])
            if s > 0:
                P.wait("gpsimd", tok_seq_end)
                P.wait("gpsimd", tok_out_prev)
            tok_x = P.dma("gpsimd", xin[:, sl:sl + 1024], x_r[s, t])
            if s == 0 and t == 0:
                x_toks_dbg = tok_x
            P.wait("scalar", tok_x)
            P.wait("vector", tok_x)
            if s > 0:
                P.waits("scalar", [tok_out_prev])
                P.waits("vector", [tok_out_prev])
            P.s(lambda e, sl=sl, t=t: e.activation(
                out=sq_scr1, in_=xin[:, sl:sl + 1024],
                func=AF.Square, accum_out=s2c[:, t:t + 1]))
            P.v(lambda e, sl=sl, t=t: e.reduce_sum(
                out=s1c[:, t:t + 1], in_=xin[:, sl:sl + 1024], axis=AX))
            ms = P.mark("scalar")
            P.wait("tensor", ms)
            beat()
            P.wait("vector", ms)
            P.drain("vector")
            P.v(lambda e, t=t: e.tensor_scalar(
                out=mcol[:, t:t + 1], in0=s1c[:, t:t + 1], scalar1=1.0 / D,
                scalar2=None, op0=ALU.mult))
            P.drain("vector")
            P.v(lambda e, t=t: e.tensor_tensor(
                out=vcol0[:, t:t + 1], in0=mcol[:, t:t + 1],
                in1=mcol[:, t:t + 1], op=ALU.mult))
            P.drain("vector")
            P.v(lambda e, t=t: e.scalar_tensor_tensor(
                out=vcol[:, t:t + 1], in0=s2c[:, t:t + 1], scalar=1.0 / D,
                in1=vcol0[:, t:t + 1], op0=ALU.mult, op1=ALU.subtract))
            mv = P.mark("vector")
            P.wait("tensor", mv)
            beat()
            P.wait("scalar", mv)
            P.s(lambda e, t=t: e.activation(
                out=rstd0[:, t:t + 1], in_=vcol[:, t:t + 1], func=AF.Sqrt,
                bias=cv[:, CV_EPS:CV_EPS + 1]))
            ms2 = P.mark("scalar")
            P.wait("vector", ms2)
            P.v(lambda e, t=t: e.reciprocal(out=rstd[:, t:t + 1],
                                            in_=rstd0[:, t:t + 1]))
            P.drain("vector")
            P.v(lambda e, t=t: e.scalar_tensor_tensor(
                out=mbias[:, t:t + 1], in0=mcol[:, t:t + 1], scalar=-1.0,
                in1=rstd[:, t:t + 1], op0=ALU.mult, op1=ALU.mult))
            mv2 = P.mark("vector")
            P.wait("tensor", mv2)
            beat()
            P.wait("scalar", mv2)
            P.s(lambda e, sl=sl, t=t: e.activation(
                out=xh_tm, in_=xin[:, sl:sl + 1024], func=AF.Identity,
                scale=rstd[:, t:t + 1], bias=mbias[:, t:t + 1]))
            norm_done[t % 2] = P.mark("scalar")
            if s == 0 and t == 0:
                dbgt("xin0", xin[:, 0:1024], [x_toks_dbg])
                dbgt("stats", s1c, [norm_done[0]])
                dbgt("stats2", s2c, [norm_done[0]])
                dbgt("rstd", rstd, [norm_done[0]])
                dbgt("mcol", mcol, [norm_done[0]])
                dbgt("vcol", vcol, [norm_done[0]])
                dbgt("rstd0", rstd0, [norm_done[0]])
                dbgt("xhtm0", xh_tm, [norm_done[0]])
            P.wait("tensor", norm_done[t % 2])
            for half in range(2):
                P.wait("tensor", ptev[half])
                for j in range(4):
                    c = half * 4 + j
                    P.t(lambda e, half=half, j=j, c=c: e.transpose(
                        pT[half][:, j * 128:(j + 1) * 128],
                        xh_tm[:, c * 128:(c + 1) * 128], identt))
            mt = P.mark("tensor")
            P.wait("scalar", mt)
            P.wait("vector", mt)
            for half in range(2):
                c0 = half * 4
                dsth = xhat_h.rearrange("p (c w) -> p c w", c=8)[
                    :, c0:c0 + 4, t * 128:(t + 1) * 128]
                dstl = xhat_l.rearrange("p (c w) -> p c w", c=8)[
                    :, c0:c0 + 4, t * 128:(t + 1) * 128]
                ptr = pT[half].rearrange("p (c w) -> p c w", c=4)
                P.split(ptr, dsth, dstl)
                ptev[half] = P.mark("vector")
        tok_xhat = P.mark("vector")
        if s == 0:
            dbgt("xhat_h", xhat_h[:, 0:2048], [tok_xhat, P.mark("scalar")])
            dbgt("xhat_l", xhat_l[:, 0:2048], [tok_xhat])
        if PHASES <= 1:
            break

        P.waits("tensor", [tok_xhat])
        ev_prev = [None, None]
        for tg in range(NTG):
            ab = tg % 2
            P.wait("tensor", ev_prev[ab])

            def u_mms(e, tg=tg, ab=ab):
                last = None
                for c in range(8):
                    wh = la_h[:, c * 32:(c + 1) * 32]
                    wl = la_l[:, c * 32:(c + 1) * 32]
                    xh = xhat_h[:, c * 2048 + tg * 512:
                                c * 2048 + (tg + 1) * 512]
                    xl = xhat_l[:, c * 2048 + tg * 512:
                                c * 2048 + (tg + 1) * 512]
                    e.matmul(pA[ab][0:32, :], wh, xh, start=(c == 0),
                             stop=False)
                    e.matmul(pA[ab][0:32, :], wh, xl, start=False, stop=False)
                    last = e.matmul(pA[ab][0:32, :], wl, xh, start=False,
                                    stop=(c == 7))
                return last
            mm_tok = P.mark_inst("tensor", u_mms)
            P.wait("scalar", mm_tok)
            P.s(lambda e, ab=ab: e.activation(
                out=ustage, in_=pA[ab][0:32, :], func=AF.Identity, scale=0.5))
            ms = P.mark("scalar")
            P.wait("vector", ms)
            P.v(lambda e, tg=tg: e.tensor_copy(
                out=u_h[:, tg * 512:(tg + 1) * 512], in_=ustage))
            P.v(lambda e, tg=tg: e.scalar_tensor_tensor(
                out=u_l[:, tg * 512:(tg + 1) * 512], in0=ustage, scalar=1.0,
                in1=u_h[:, tg * 512:(tg + 1) * 512], op0=ALU.mult,
                op1=ALU.subtract))
            ev_prev[ab] = P.mark("vector")
        tok_u = P.mark("vector")
        if s == 0:
            dbgt("u_h", u_h, [tok_u])
            dbgt("u_l", u_l, [tok_u])
        if PHASES <= 2:
            break

        pair_exp = None
        pv_prev = None
        sp_prev = None
        ynorm_done = [None, None]
        for g in range(NHG):
            P.waits("tensor", [tok_xhat, tok_u])
            P.wait("tensor", ynorm_done[0])
            P.wait("tensor", ynorm_done[1])
            qkv_names = [(0, qg_h, qg_l), (1, kg_h, kg_l), (2, None, None)]
            for pi in range(3):
                wcol = pi * D + g * 128
                wh_half, wtok = ws_load(
                    wslice(wqkv_h, wqkv_l, 0, 8, wcol // 128))
                lb_half, lbtok = lb_load(wcol)
                P.waits("tensor", wtok + [lbtok])
                last_mm = None
                for tg in range(NTG):
                    ab = tg % 2
                    P.wait("tensor", ev_prev[ab])

                    def qkv_mms(e, wh_half=wh_half, lb_half=lb_half, tg=tg,
                                ab=ab):
                        off = wh_half * 1024
                        for c in range(8):
                            wh = ws_h[:, off + c * 128: off + (c + 1) * 128]
                            wl = ws_l[:, off + c * 128: off + (c + 1) * 128]
                            xh = xhat_h[:, c * 2048 + tg * 512:
                                        c * 2048 + (tg + 1) * 512]
                            xl = xhat_l[:, c * 2048 + tg * 512:
                                        c * 2048 + (tg + 1) * 512]
                            e.matmul(pA[ab], wh, xh, start=(c == 0),
                                     stop=False)
                            e.matmul(pA[ab], wh, xl, start=False, stop=False)
                            e.matmul(pA[ab], wl, xh, start=False, stop=False)
                        lo = lb_half * 128
                        lh = lbs_h[:, lo:lo + 128]
                        ll = lbs_l[:, lo:lo + 128]
                        uh = u_h[:, tg * 512:(tg + 1) * 512]
                        ul = u_l[:, tg * 512:(tg + 1) * 512]
                        e.matmul(pA[ab], lh, uh, start=False, stop=False)
                        e.matmul(pA[ab], lh, ul, start=False, stop=False)
                        return e.matmul(pA[ab], ll, uh, start=False, stop=True)
                    mm_tok = P.mark_inst("tensor", qkv_mms)
                    last_mm = mm_tok
                    bias_col = bat[:, pi * 8 + g: pi * 8 + g + 1]
                    if pi < 2:
                        dh = qkv_names[pi][1][:, tg * 512:(tg + 1) * 512]
                        dl = qkv_names[pi][2][:, tg * 512:(tg + 1) * 512]
                        P.wait("scalar", mm_tok)

                        def hi_ev(e, dh=dh, ab=ab, bias_col=bias_col):
                            e.activation(out=dh, in_=pA[ab], func=AF.Identity,
                                         bias=bias_col)
                        P.s(hi_ev)
                        ms = P.mark("scalar")
                        P.wait("vector", ms)
                        P.v(lambda e, dl=dl, ab=ab, bias_col=bias_col, dh=dh:
                            e.scalar_tensor_tensor(
                                out=dl, in0=pA[ab], scalar=bias_col, in1=dh,
                                op0=ALU.add, op1=ALU.subtract))
                        ev_prev[ab] = P.mark("vector")
                    else:
                        P.wait("scalar", mm_tok)
                        P.s(lambda e, ab=ab, bias_col=bias_col: e.activation(
                            out=vstage, in_=pA[ab], func=AF.Identity,
                            bias=bias_col))
                        ms = P.mark("scalar")
                        ev_prev[ab] = ms
                        P.wait("tensor", ms)
                        ph = pT[tg % 2]
                        P.wait("tensor", ptev[tg % 2])
                        for j in range(4):
                            P.t(lambda e, ph=ph, j=j: e.transpose(
                                ph[:, j * 128:(j + 1) * 128],
                                vstage[:, j * 128:(j + 1) * 128], identt))
                        mt = P.mark("tensor")
                        P.wait("scalar", mt)
                        P.wait("vector", mt)
                        dsth = vt_h[:, tg * 512:(tg + 1) * 512]
                        dstl = vt_l[:, tg * 512:(tg + 1) * 512]
                        P.split(ph, dsth, dstl)
                        ptev[tg % 2] = P.mark("vector")
                ws_done(wh_half, last_mm)
                lb_done(lb_half, last_mm)
            tok_qkv = [P.mark("scalar"), P.mark("vector")]
            if s == 0 and g == 0:
                dbgt("qg_h", qg_h, tok_qkv)
                dbgt("qg_l", qg_l, tok_qkv)
                dbgt("kg_h", kg_h, tok_qkv)
                dbgt("vt_h", vt_h, tok_qkv)
                dbgt("vt_l", vt_l, tok_qkv)

            P.waits("tensor", tok_qkv)
            for h2 in range(2):
                hp = 64 * h2
                for qb in range(NTG):
                    nkt = 4 * (qb + 1)
                    npair = nkt // 2
                    yi = (h2 * NTG + qb) % 2
                    P.wait("tensor", ynorm_done[yi])

                    def emit_pv(jp, sp_tok, yi=yi, hp=hp, nkt=nkt):
                        P.wait("tensor", sp_tok)
                        tp = (0, hp) if hp else None

                        def pvmms(e, hp=hp, jp=jp, yi=yi, nkt=nkt, tp=tp):
                            last = None
                            for u in range(2):
                                j = 2 * jp + u
                                po = u * 512
                                vh = vt_h[:, j * 128 + hp: j * 128 + hp + 64]
                                vl = vt_l[:, j * 128 + hp: j * 128 + hp + 64]
                                ph_ = pt_h[:, po:po + 512]
                                pl_ = pt_l[:, po:po + 512]
                                st = (j == 0)
                                yps = pY[yi][hp:hp + 64, :]
                                sps = pA[yi][0:1, :]
                                e.matmul(yps, vh, ph_, start=st, stop=False,
                                         tile_position=tp)
                                e.matmul(yps, vh, pl_, start=False,
                                         stop=False, tile_position=tp)
                                e.matmul(yps, vl, ph_, start=False,
                                         stop=(j == nkt - 1),
                                         tile_position=tp)
                                e.matmul(sps, ones_bf, ph_, start=st,
                                         stop=False)
                                last = e.matmul(sps, ones_bf, pl_,
                                                start=False,
                                                stop=(j == nkt - 1))
                            return last
                        return P.mark_inst("tensor", pvmms)

                    for jp in range(npair):
                        kt0 = 2 * jp
                        P.wait("tensor", pair_exp)

                        def smms(e, hp=hp, kt0=kt0, qb=qb):
                            last = None
                            for u in range(2):
                                kt = kt0 + u
                                kh = kg_h[hp:hp + 64,
                                          kt * 128:(kt + 1) * 128]
                                kl = kg_l[hp:hp + 64,
                                          kt * 128:(kt + 1) * 128]
                                qh = qg_h[hp:hp + 64,
                                          qb * 512:(qb + 1) * 512]
                                ql = qg_l[hp:hp + 64,
                                          qb * 512:(qb + 1) * 512]
                                e.matmul(pS[u], kh, qh, start=True,
                                         stop=False)
                                e.matmul(pS[u], kh, ql, start=False,
                                         stop=False)
                                last = e.matmul(pS[u], kl, qh, start=False,
                                                stop=True)
                            return last
                        mm_tok = P.mark_inst("tensor", smms)
                        if jp >= 1:
                            pv_prev = emit_pv(jp - 1, sp_prev)
                        pre_exp = mm_tok
                        if kt0 >= 4 * qb:
                            P.wait("vector", mm_tok)
                            for u in range(2):
                                r = kt0 + u - 4 * qb
                                mc0 = 384 - 128 * r
                                P.v(lambda e, u=u, mc0=mc0: e.tensor_tensor(
                                    out=pS[u], in0=pS[u],
                                    in1=maskt[:, mc0:mc0 + 512], op=ALU.add))
                            pre_exp = P.mark("vector")
                        P.wait("scalar", pre_exp)
                        P.wait("scalar", sp_prev)
                        for u in range(2):
                            eb = ef32[u]
                            P.s(lambda e, u=u, eb=eb: e.activation(
                                out=eb, in_=pS[u], func=AF.Exp, scale=0.125,
                                bias=cv[:, CV_EXP:CV_EXP + 1]))
                        pair_exp = P.mark("scalar")
                        P.wait("vector", pair_exp)
                        P.wait("vector", pv_prev)
                        for u in range(2):
                            eb = ef32[u]
                            po = u * 512
                            P.v(lambda e, eb=eb, po=po: e.tensor_copy(
                                out=pt_h[:, po:po + 512], in_=eb))
                        for u in range(2):
                            eb = ef32[u]
                            po = u * 512
                            P.v(lambda e, eb=eb, po=po:
                                e.scalar_tensor_tensor(
                                    out=pt_l[:, po:po + 512], in0=eb,
                                    scalar=1.0, in1=pt_h[:, po:po + 512],
                                    op0=ALU.mult, op1=ALU.subtract))
                        sp_prev = P.mark("vector")
                    pv_prev = emit_pv(npair - 1, sp_prev)

                    P.wait("vector", pv_prev)
                    P.v(lambda e, yi=yi: e.reciprocal(
                        out=rrow, in_=pA[yi][0:1, :]))
                    mr = P.mark("vector")
                    P.wait("tensor", mr)
                    P.wait("tensor", ptev[0])
                    bc_tok = P.mark_inst("tensor", lambda e: e.matmul(
                        pT[0], ones_row, rrow, start=True, stop=True))
                    P.wait("scalar", bc_tok)
                    P.s(lambda e: e.activation(out=rb64, in_=pT[0],
                                               func=AF.Identity))
                    mbc = P.mark("scalar")
                    P.wait("vector", mbc)
                    P.v(lambda e, yi=yi, hp=hp: e.tensor_tensor(
                        out=ystage[hp:hp + 64, :], in0=pY[yi][hp:hp + 64, :],
                        in1=rb64[hp:hp + 64, :], op=ALU.mult))
                    yh = y_h[hp:hp + 64, g * 2048 + qb * 512:
                             g * 2048 + (qb + 1) * 512]
                    yl = y_l[hp:hp + 64, g * 2048 + qb * 512:
                             g * 2048 + (qb + 1) * 512]
                    P.v(lambda e, yh=yh, hp=hp: e.tensor_copy(
                        out=yh, in_=ystage[hp:hp + 64, :]))
                    P.v(lambda e, yh=yh, yl=yl, hp=hp: e.scalar_tensor_tensor(
                        out=yl, in0=ystage[hp:hp + 64, :], scalar=1.0, in1=yh,
                        op0=ALU.mult, op1=ALU.subtract))
                    ynorm_done[yi] = P.mark("vector")
                    ptev[0] = ynorm_done[yi]

        tok_y = P.mark("vector")
        if s == 0:
            dbgt("y_h", y_h[:, 0:4096], [tok_y])
            dbgt("y_l", y_l[:, 0:4096], [tok_y])
        if PHASES <= 3:
            break

        out_dma_toks = [None, None]
        pstage_guard = None
        xg = [None, None]
        P.wait("gpsimd", tok_y)
        for tg in range(NTG):
            P.waits("tensor", [tok_y])

            def u2_mms(e, tg=tg):
                last = None
                for c in range(8):
                    wh = lap_h[:, c * 32:(c + 1) * 32]
                    wl = lap_l[:, c * 32:(c + 1) * 32]
                    yh = y_h[:, c * 2048 + tg * 512: c * 2048 + (tg + 1) * 512]
                    yl = y_l[:, c * 2048 + tg * 512: c * 2048 + (tg + 1) * 512]
                    e.matmul(pA[0][0:32, :], wh, yh, start=(c == 0),
                             stop=False)
                    e.matmul(pA[0][0:32, :], wh, yl, start=False, stop=False)
                    last = e.matmul(pA[0][0:32, :], wl, yh, start=False,
                                    stop=(c == 7))
                return last
            mm_tok = P.mark_inst("tensor", u2_mms)
            P.wait("scalar", mm_tok)
            P.s(lambda e: e.activation(out=ustage, in_=pA[0][0:32, :],
                                       func=AF.Identity, scale=0.5))
            ms = P.mark("scalar")
            P.wait("vector", ms)
            P.v(lambda e: e.tensor_copy(out=u2_h, in_=ustage))
            P.v(lambda e: e.scalar_tensor_tensor(
                out=u2_l, in0=ustage, scalar=1.0, in1=u2_h, op0=ALU.mult,
                op1=ALU.subtract))
            mu2 = P.mark("vector")
            P.wait("tensor", mu2)

            pr_ev = [None, None]
            for oc in range(8):
                ab = oc % 2
                whalf, wtok = ws_load(
                    wslice(wproj_h, wproj_l, 0, 8, oc))
                P.waits("tensor", wtok)
                P.wait("tensor", pr_ev[ab])

                def proj_all(e, whalf=whalf, tg=tg, ab=ab, oc=oc):
                    off = whalf * 1024
                    for c in range(8):
                        wh = ws_h[:, off + c * 128: off + (c + 1) * 128]
                        wl = ws_l[:, off + c * 128: off + (c + 1) * 128]
                        yh = y_h[:, c * 2048 + tg * 512:
                                 c * 2048 + (tg + 1) * 512]
                        yl = y_l[:, c * 2048 + tg * 512:
                                 c * 2048 + (tg + 1) * 512]
                        e.matmul(pA[ab], wh, yh, start=(c == 0), stop=False)
                        e.matmul(pA[ab], wh, yl, start=False, stop=False)
                        e.matmul(pA[ab], wl, yh, start=False, stop=False)
                    lh = lbp_h[:, oc * 128:(oc + 1) * 128]
                    ll = lbp_l[:, oc * 128:(oc + 1) * 128]
                    e.matmul(pA[ab], lh, u2_h, start=False, stop=False)
                    e.matmul(pA[ab], lh, u2_l, start=False, stop=False)
                    return e.matmul(pA[ab], ll, u2_h, start=False, stop=True)
                mm_tok = P.mark_inst("tensor", proj_all)
                ws_done(whalf, mm_tok)
                P.wait("scalar", mm_tok)
                P.wait("scalar", pstage_guard)
                P.s(lambda e, ab=ab, oc=oc: e.activation(
                    out=pstage[:, oc * 512:(oc + 1) * 512], in_=pA[ab],
                    func=AF.Identity, bias=bpt[:, oc:oc + 1]))
                pr_ev[ab] = P.mark("scalar")
            tok_proj = P.mark("scalar")
            if s == 0 and tg == 0:
                dbgt("pstage", pstage, [tok_proj])
                dbgt("u2_h", u2_h, [tok_proj])

            P.waits("tensor", [tok_proj])
            for j in range(4):
                P.wait("gpsimd", xg[j % 2])
                tok_xr = P.dma("gpsimd", xres[:, (j % 2) * 1024:
                                               (j % 2) * 1024 + 1024],
                               x_r[s, tg * 4 + j])
                for half in range(2):
                    P.wait("tensor", ptev[half])
                    for jo in range(4):
                        oc = half * 4 + jo
                        P.t(lambda e, half=half, jo=jo, oc=oc, j=j:
                            e.transpose(pT[half][:, jo * 128:(jo + 1) * 128],
                                        pstage[:, oc * 512 + j * 128:
                                               oc * 512 + (j + 1) * 128],
                                        identt))
                mt = P.mark("tensor")
                P.wait("vector", mt)
                P.wait("vector", tok_xr)
                for half in range(2):
                    P.v(lambda e, half=half, j=j: e.tensor_tensor(
                        out=x1stg[:, half * 512:(half + 1) * 512],
                        in0=pT[half],
                        in1=xres[:, (j % 2) * 1024 + half * 512:
                                 (j % 2) * 1024 + (half + 1) * 512],
                        op=ALU.add))
                mv0 = P.mark("vector")
                P.wait("scalar", mv0)
                tix = (tg * 4 + j) * 1024
                P.split(x1stg, x1h[:, tix:tix + 1024], x1l[:, tix:tix + 1024])
                mvs = P.mark("vector")
                ptev[0] = mvs
                ptev[1] = mvs
                xg[j % 2] = mvs
            pstage_guard = P.mark("tensor")
        tok_x1all = [P.mark("scalar"), P.mark("vector")]
        if s == 0:
            dbgt("x1h", x1h[:, 0:4096], tok_x1all)
            dbgt("x1l", x1l[:, 0:4096], tok_x1all)

        ffn_done = None
        for tg in range(NTG):
            P.waits("scalar", tok_x1all)
            P.waits("vector", tok_x1all)
            for j in range(4):
                tix = (tg * 4 + j) * 1024
                if j == 0:
                    P.waits("vector", out_dma_toks)
                P.v(lambda e, tix=tix: e.tensor_tensor(
                    out=xfull, in0=x1h[:, tix:tix + 1024],
                    in1=x1l[:, tix:tix + 1024], op=ALU.add))
                mvx = P.mark("vector")
                P.wait("scalar", mvx)
                P.s(lambda e, j=j: e.activation(
                    out=sq_ln, in_=xfull, func=AF.Square,
                    accum_out=s2c[:, j:j + 1]))
                P.v(lambda e, j=j: e.reduce_sum(
                    out=s1c[:, j:j + 1], in_=xfull, axis=AX))
                msq = P.mark("scalar")
                P.wait("tensor", msq)
                beat()
                P.wait("vector", msq)
                P.drain("vector")
                P.v(lambda e, j=j: e.tensor_scalar(
                    out=mcol[:, j:j + 1], in0=s1c[:, j:j + 1],
                    scalar1=1.0 / D, scalar2=None, op0=ALU.mult))
                P.drain("vector")
                P.v(lambda e, j=j: e.tensor_tensor(
                    out=vcol0[:, j:j + 1], in0=mcol[:, j:j + 1],
                    in1=mcol[:, j:j + 1], op=ALU.mult))
                P.drain("vector")
                P.v(lambda e, j=j: e.scalar_tensor_tensor(
                    out=vcol[:, j:j + 1], in0=s2c[:, j:j + 1],
                    scalar=1.0 / D, in1=vcol0[:, j:j + 1], op0=ALU.mult,
                    op1=ALU.subtract))
                mvc = P.mark("vector")
                P.wait("tensor", mvc)
                beat()
                P.wait("scalar", mvc)
                P.s(lambda e, j=j: e.activation(
                    out=rstd0[:, j:j + 1], in_=vcol[:, j:j + 1], func=AF.Sqrt,
                    bias=cv[:, CV_EPS:CV_EPS + 1]))
                msc2 = P.mark("scalar")
                P.wait("vector", msc2)
                P.v(lambda e, j=j: e.reciprocal(out=rstd[:, j:j + 1],
                                                in_=rstd0[:, j:j + 1]))
                P.drain("vector")
                P.v(lambda e, j=j: e.scalar_tensor_tensor(
                    out=mbias[:, j:j + 1], in0=mcol[:, j:j + 1], scalar=-1.0,
                    in1=rstd[:, j:j + 1], op0=ALU.mult, op1=ALU.mult))
                mvc2 = P.mark("vector")
                P.wait("tensor", mvc2)
                beat()
                P.wait("scalar", mvc2)
                P.s(lambda e, j=j: e.activation(
                    out=xh_tm, in_=xfull, func=AF.Identity,
                    scale=rstd[:, j:j + 1], bias=mbias[:, j:j + 1]))
                msx = P.mark("scalar")
                P.wait("tensor", msx)
                for half in range(2):
                    P.wait("tensor", ptev[half])
                    for jo in range(4):
                        c = half * 4 + jo
                        P.t(lambda e, half=half, jo=jo, c=c: e.transpose(
                            pT[half][:, jo * 128:(jo + 1) * 128],
                            xh_tm[:, c * 128:(c + 1) * 128], identt))
                mt = P.mark("tensor")
                P.wait("scalar", mt)
                P.wait("vector", mt)
                for half in range(2):
                    c0 = half * 4
                    dsth = xh1_h.rearrange("p (c w) -> p c w", c=8)[
                        :, c0:c0 + 4, j * 128:(j + 1) * 128]
                    dstl = xh1_l.rearrange("p (c w) -> p c w", c=8)[
                        :, c0:c0 + 4, j * 128:(j + 1) * 128]
                    ptr = pT[half].rearrange("p (c w) -> p c w", c=4)
                    P.split(ptr, dsth, dstl)
                    ptev[half] = P.mark("vector")
            tok_xh1 = [P.mark("scalar"), P.mark("vector")]

            P.waits("tensor", tok_xh1)
            gel_ev = None
            z_done = [None, None]
            hbank = [pA[0], pA[1], pS[0], pS[1]]
            SETS = [GB, GB2]
            ZS = [GB[6], GB[7]]
            TS = [GB2[6], GB2[7]]
            for fp in range(16):
                f0 = 2 * fp
                bp = fp % 2
                P.wait("tensor", z_done[bp])
                mm_tok = None
                for u in range(2):
                    f = f0 + u
                    whalf, wtok = ws_load(wslice(w1_hs, w1_ls, 0, 8, f))
                    P.waits("tensor", wtok)
                    pb = hbank[bp * 2 + u]

                    def h_mms(e, whalf=whalf, pb=pb):
                        off = whalf * 1024
                        last = None
                        for c in range(8):
                            wh = ws_h[:, off + c * 128: off + (c + 1) * 128]
                            wl = ws_l[:, off + c * 128: off + (c + 1) * 128]
                            xh = xh1_h[:, c * 512:(c + 1) * 512]
                            xl = xh1_l[:, c * 512:(c + 1) * 512]
                            e.matmul(pb, wh, xh, start=(c == 0), stop=False)
                            e.matmul(pb, wh, xl, start=False, stop=False)
                            last = e.matmul(pb, wl, xh, start=False,
                                            stop=(c == 7))
                        return last
                    mm_tok = P.mark_inst("tensor", h_mms)
                    ws_done(whalf, mm_tok)
                P.wait("scalar", mm_tok)
                P.wait("scalar", gel_ev)
                for u in range(2):
                    pb = hbank[bp * 2 + u]
                    P.s(lambda e, pb=pb, u=u, f0=f0: e.activation(
                        out=ZS[u], in_=pb, func=AF.Identity,
                        scale=256.0, bias=b1t[:, f0 + u:f0 + u + 1]))
                if fp >= 1:
                    for u in range(2):
                        fprev = f0 - 2 + u
                        ghp = hgel_h[:, fprev * 512:(fprev + 1) * 512]
                        P.s(lambda e, ghp=ghp, W=SETS[u][2]: e.activation(
                            out=ghp, in_=W, func=AF.Identity,
                            scale=1.0 / 256.0))
                z_tok = P.mark("scalar")
                z_done[bp] = z_tok
                P.wait("vector", z_tok)
                if fp == 0:
                    P.waits("vector", out_dma_toks)
                if fp >= 1:
                    for u in range(2):
                        fprev = f0 - 2 + u
                        ghp = hgel_h[:, fprev * 512:(fprev + 1) * 512]
                        glp = hgel_l[:, fprev * 512:(fprev + 1) * 512]
                        P.v(lambda e, ghp=ghp, glp=glp, W=SETS[u][2]:
                            e.scalar_tensor_tensor(
                                out=glp, in0=W, scalar=1.0 / 256.0,
                                in1=ghp, op0=ALU.mult, op1=ALU.subtract))
                for u in range(2):
                    S = SETS[u]
                    P.v(lambda e, S=S, zb=ZS[u]: e.tensor_scalar(
                        out=S[0], in0=zb, scalar1=-1024.0, scalar2=1024.0,
                        op0=ALU.max, op1=ALU.min))
                    P.rnd(S[3], S[0], S[1], S[2])
                    P.v(lambda e, S=S: e.tensor_scalar(
                        out=S[1], in0=S[0], scalar1=31.5,
                        scalar2=1.0 / 64.0, op0=ALU.subtract, op1=ALU.mult))
                    P.rnd(S[4], S[1], S[2], S[0])
                    P.v(lambda e, S=S: e.scalar_tensor_tensor(
                        out=S[2], in0=S[4], scalar=-64.0, in1=S[3],
                        op0=ALU.mult, op1=ALU.add))
                mv = P.mark("vector")
                P.wait("scalar", mv)
                for u in range(2):
                    S = SETS[u]
                    P.s(lambda e, S=S: e.activation(
                        out=S[1], in_=S[4], func=AF.Gelu, scale=0.25))
                    P.s(lambda e, S=S: e.activation(
                        out=S[0], in_=S[4], func=AF.Gelu, scale=0.25,
                        bias=cv[:, CV_Y1:CV_Y1 + 1]))
                    P.s(lambda e, zb=ZS[u], u=u: e.activation(
                        out=TS[u], in_=zb, func=AF.Relu,
                        bias=cv[:, CV_NT:CV_NT + 1]))
                msg = P.mark("scalar")
                P.wait("vector", msg)
                for u in range(2):
                    S = SETS[u]
                    P.v(lambda e, S=S: e.tensor_tensor(
                        out=S[5], in0=S[0], in1=S[1], op=ALU.subtract))
                    P.v(lambda e, S=S: e.tensor_tensor(
                        out=S[3], in0=S[2], in1=S[5], op=ALU.mult))
                    P.v(lambda e, S=S: e.scalar_tensor_tensor(
                        out=S[2], in0=S[3], scalar=1.0 / 64.0, in1=S[1],
                        op0=ALU.mult, op1=ALU.add))
                    P.v(lambda e, S=S, u=u: e.scalar_tensor_tensor(
                        out=S[1], in0=S[2], scalar=256.0, in1=TS[u],
                        op0=ALU.mult, op1=ALU.add))
                    P.rnd(S[2], S[1], S[0], S[3])
                gel_ev = P.mark("vector")
            P.wait("scalar", gel_ev)
            for u in range(2):
                fprev = 30 + u
                ghp = hgel_h[:, fprev * 512:(fprev + 1) * 512]
                P.s(lambda e, ghp=ghp, W=SETS[u][2]: e.activation(
                    out=ghp, in_=W, func=AF.Identity, scale=1.0 / 256.0))
            fl_tok = P.mark("scalar")
            P.wait("vector", fl_tok)
            for u in range(2):
                fprev = 30 + u
                ghp = hgel_h[:, fprev * 512:(fprev + 1) * 512]
                glp = hgel_l[:, fprev * 512:(fprev + 1) * 512]
                P.v(lambda e, ghp=ghp, glp=glp, W=SETS[u][2]:
                    e.scalar_tensor_tensor(
                        out=glp, in0=W, scalar=1.0 / 256.0, in1=ghp,
                        op0=ALU.mult, op1=ALU.subtract))
            tok_hgel = P.mark("vector")
            if s == 0 and tg == 0:
                dbgt("hgel_h", hgel_h[:, 0:2048], [tok_hgel])
                dbgt("xh1_h", xh1_h[:, 0:1024], [tok_hgel])

            P.waits("tensor", [tok_hgel])
            oq_ev = [None, None]
            o_last = None
            for oc in range(8):
                ab = oc % 2
                P.wait("tensor", oq_ev[ab])
                mmq = None
                for q in range(4):
                    hq, wtokq = ws_load(
                        wslice(w2_hs, w2_ls, q * 8, (q + 1) * 8, oc))
                    P.waits("tensor", wtokq)

                    def o_mms(e, hq=hq, ab=ab, q=q):
                        off = hq * 1024
                        last = None
                        for ci in range(8):
                            c = q * 8 + ci
                            wh = ws_h[:, off + ci * 128: off + (ci + 1) * 128]
                            wl = ws_l[:, off + ci * 128: off + (ci + 1) * 128]
                            hh = hgel_h[:, c * 512:(c + 1) * 512]
                            hl = hgel_l[:, c * 512:(c + 1) * 512]
                            e.matmul(pA[ab], wh, hh, start=(c == 0),
                                     stop=False)
                            e.matmul(pA[ab], wh, hl, start=False, stop=False)
                            last = e.matmul(pA[ab], wl, hh, start=False,
                                            stop=(c == 31))
                        return last
                    mmq = P.mark_inst("tensor", o_mms)
                    ws_done(hq, mmq)
                o_last = mmq
                P.wait("scalar", mmq)
                P.s(lambda e, ab=ab, oc=oc: e.activation(
                    out=GB[4], in_=pA[ab], func=AF.Identity, scale=256.0,
                    bias=b2t[:, oc:oc + 1]))
                mso = P.mark("scalar")
                P.wait("vector", mso)
                P.rnd(GB[5], GB[4], GB[6], GB[7])
                P.v(lambda e, oc=oc: e.tensor_scalar(
                    out=ofm[:, oc * 512:(oc + 1) * 512], in0=GB[5],
                    scalar1=1.0 / 256.0, scalar2=None, op0=ALU.mult))
                oq_ev[ab] = P.mark("vector")
            ffn_done = o_last
            tok_ofm = P.mark("vector")

            P.waits("tensor", [tok_ofm])
            for j in range(4):
                for half in range(2):
                    P.wait("tensor", ptev[half])
                    for jo in range(4):
                        oc = half * 4 + jo
                        P.t(lambda e, half=half, jo=jo, oc=oc, j=j:
                            e.transpose(pT[half][:, jo * 128:(jo + 1) * 128],
                                        ofm[:, oc * 512 + j * 128:
                                            oc * 512 + (j + 1) * 128],
                                        identt))
                mt = P.mark("tensor")
                P.wait("vector", mt)
                osl = (j % 2) * 1024
                P.wait("vector", out_dma_toks[j % 2])
                tix = (tg * 4 + j) * 1024
                for half in range(2):
                    P.v(lambda e, half=half, tix=tix, osl=osl:
                        e.scalar_tensor_tensor(
                            out=ot_st[:, osl + half * 512:
                                      osl + (half + 1) * 512],
                            in0=x1h[:, tix + half * 512:
                                    tix + (half + 1) * 512],
                            scalar=1.0, in1=pT[half], op0=ALU.mult,
                            op1=ALU.add))
                    P.v(lambda e, half=half, tix=tix, osl=osl:
                        e.tensor_tensor(
                            out=ot_st[:, osl + half * 512:
                                      osl + (half + 1) * 512],
                            in0=ot_st[:, osl + half * 512:
                                      osl + (half + 1) * 512],
                            in1=x1l[:, tix + half * 512:
                                    tix + (half + 1) * 512],
                            op=ALU.add))
                mv = P.mark("vector")
                ptev[0] = mv
                ptev[1] = mv
                P.wait("sync", mv)
                out_dma_toks[j % 2] = P.dma("sync", out_r[s, tg * 4 + j],
                                            ot_st[:, osl:osl + 1024])

        tok_seq_end = P.mark("tensor")
        tok_out_prev = out_dma_toks[1]

    P.wait("sync", ("d_sync", P.cnt["d_sync"]))
    P.wait("gpsimd", ("d_gpsimd", P.cnt["d_gpsimd"]))

    with stack:
        with nc.Block() as block:
            P.emit(block)

    return nc


def host_inputs(L0_attn_x, L0_attn_w_attn, L0_attn_b_attn, L0_attn_la_attn,
                L0_attn_lb_attn, L0_attn_w_proj, L0_attn_b_proj,
                L0_attn_la_proj, L0_attn_lb_proj, L0_ffn_w1, L0_ffn_b1,
                L0_ffn_w2, L0_ffn_b2):
    C = np.ascontiguousarray
    asn = np.asarray
    wqkvT = C(asn(L0_attn_w_attn).T)
    battn = C(asn(L0_attn_b_attn).reshape(24, 128).T)
    laTa = C(asn(L0_attn_la_attn).T.reshape(8, 128, 32).transpose(1, 0, 2)
             .reshape(128, 256))
    lbTa = C(asn(L0_attn_lb_attn).T)
    wprojT = C(asn(L0_attn_w_proj).T)
    bproj = C(asn(L0_attn_b_proj).reshape(8, 128).T)
    laTp = C(asn(L0_attn_la_proj).T.reshape(8, 128, 32).transpose(1, 0, 2)
             .reshape(128, 256))
    lbTp = C(asn(L0_attn_lb_proj).T)
    w1 = C(asn(L0_ffn_w1))
    b1c = C(asn(L0_ffn_b1).reshape(32, 128).T)
    w2 = C(asn(L0_ffn_w2))
    b2c = C(asn(L0_ffn_b2).reshape(8, 128).T)

    identx = np.eye(128, dtype=np.float32)
    u = np.arange(896)[None, :]
    p = np.arange(128)[:, None]
    import ml_dtypes
    maskx = np.where(u >= p + 384, 0.0, -1e30).astype(ml_dtypes.bfloat16)
    cvec = np.zeros((128, 8), np.float32)
    cvec[:, 0] = 0.25
    cvec[:, 1] = -12.0
    cvec[:, 2] = 1e-5
    cvec[:, 3] = -1024.0

    x = asn(L0_attn_x)
    shared = dict(wqkvT=wqkvT, battn=battn, laTa=laTa, lbTa=lbTa,
                  wprojT=wprojT, bproj=bproj, laTp=laTp, lbTp=lbTp,
                  w1=w1, b1c=b1c, w2=w2, b2c=b2c, identx=identx,
                  maskx=maskx, cvec=cvec)
    in_maps = []
    for core in range(8):
        d = dict(shared)
        d["x"] = C(x[2 * core: 2 * core + 2])
        in_maps.append(d)
    return in_maps


_NC_CACHE = {}


def kernel(**inputs):
    in_maps = host_inputs(**inputs)
    if "nc" not in _NC_CACHE:
        _NC_CACHE["nc"] = build()
    nc = _NC_CACHE["nc"]
    res = run_bass_kernel_spmd(nc, in_maps, core_ids=list(range(8)))
    out = np.concatenate([np.asarray(r["out"]) for r in res.results], axis=0)
    return out.astype(np.float32)


if __name__ == "__main__":
    nc = build()
    n = sum(1 for _ in nc.all_instructions())
    print("instructions:", n)


# version 16
# speedup vs baseline: 1.1300x; 1.0184x over previous
import contextlib
import os
import numpy as np
import concourse.bass as bass
import concourse.mybir as mybir
from concourse.bass_utils import run_bass_kernel_spmd

F32 = mybir.dt.float32
BF16 = mybir.dt.bfloat16
AF = mybir.ActivationFunctionType
ALU = mybir.AluOpType
AX = mybir.AxisListType.X

C23 = float(3 * 2.0**22)
C52 = float(3 * 2.0**51)
C36 = float(3 * 2.0**35)
C37 = float(3 * 2.0**36)

T = 2048
D = 1024
NSEQ = 2
NTILE = 16
NTG = 4
NHG = 8
FF = 4096

ENGINES = ("sync", "gpsimd", "tensor", "vector", "scalar")


class Prog:
    def __init__(self, nc):
        self.nc = nc
        self.ops = {e: [] for e in ENGINES}
        self.cnt = {}
        self.seen = {e: {} for e in ENGINES}
        self.sems = {}

    def make_sems(self, stack):
        for n in ["d_sync", "d_gpsimd", "m_tensor", "m_vector", "m_scalar",
                  "m_gpsimd"]:
            self.sems[n] = stack.enter_context(self.nc.semaphore(n))
            self.cnt[n] = 0

    def op(self, eng, fn):
        self.ops[eng].append(fn)

    def dma(self, q, out, in_):
        s = "d_" + q
        self.cnt[s] += 16
        v = self.cnt[s]
        sem = self.sems[s]

        def f(e, out=out, in_=in_, sem=sem):
            e.dma_start(out=out, in_=in_).then_inc(sem, 16)
        self.op(q, f)
        return (s, v)

    def wait(self, eng, tok):
        if tok is None:
            return
        s, v = tok
        if self.seen[eng].get(s, 0) >= v:
            return
        self.seen[eng][s] = v
        sem = self.sems[s]
        self.op(eng, lambda e, sem=sem, v=v: e.wait_ge(sem, v))

    def waits(self, eng, toks):
        for t in toks:
            self.wait(eng, t)

    def mark(self, eng):
        s = "m_" + eng
        self.cnt[s] += 1
        v = self.cnt[s]
        sem = self.sems[s]
        self.op(eng, lambda e, sem=sem: e.maybe_drain_then_inc((sem, 1)))
        return (s, v)

    def mark_inst(self, eng, fn):
        s = "m_" + eng
        self.cnt[s] += 1
        v = self.cnt[s]
        sem = self.sems[s]

        def f(e, fn=fn, sem=sem):
            fn(e).then_inc(sem, 1)
        self.op(eng, f)
        return (s, v)

    def t(self, fn):
        self.op("tensor", fn)

    def v(self, fn):
        self.op("vector", fn)

    def s(self, fn):
        self.op("scalar", fn)

    def g(self, fn):
        self.op("gpsimd", fn)

    def drain(self, eng):
        self.op(eng, lambda e: e.drain())

    def rnd(self, out, in_, t1, t2, cbig=None):
        nflag = self.nflag_col

        def f(e, out=out, in_=in_, t1=t1, t2=t2, nflag=nflag):
            e.tensor_scalar(out=t1, in0=in_, scalar1=C23, scalar2=C23,
                            op0=ALU.add, op1=ALU.subtract)
            e.tensor_scalar(out=t2, in0=in_, scalar1=C52, scalar2=C52,
                            op0=ALU.add, op1=ALU.subtract)
            e.scalar_tensor_tensor(out=t2, in0=in_, scalar=nflag, in1=t2,
                                   op0=ALU.mult, op1=ALU.add)
            e.tensor_tensor(out=out, in0=t1, in1=t2, op=ALU.add)
        self.op("vector", f)

    def split(self, src, hi, lo, bias=None):
        if bias is None:
            self.s(lambda e, src=src, hi=hi: e.activation(
                out=hi, in_=src, func=AF.Identity))
        else:
            self.s(lambda e, src=src, hi=hi, bias=bias: e.activation(
                out=hi, in_=src, func=AF.Identity, bias=bias))
        ms = self.mark("scalar")
        self.wait("vector", ms)
        if bias is None:
            self.v(lambda e, src=src, hi=hi, lo=lo: e.scalar_tensor_tensor(
                out=lo, in0=src, scalar=1.0, in1=hi,
                op0=ALU.mult, op1=ALU.subtract))
        else:
            self.v(lambda e, src=src, hi=hi, lo=lo, bias=bias:
                   e.scalar_tensor_tensor(out=lo, in0=src, scalar=bias,
                                          in1=hi, op0=ALU.add,
                                          op1=ALU.subtract))

    def emit(self, block):
        for eng in ENGINES:
            ops = self.ops[eng]
            deco = getattr(block, eng)

            def body(e, ops=ops):
                for f in ops:
                    f(e)
            deco(body)


def build():
    nc = bass.Bass("TRN2", target_bir_lowering=False,
                   detect_race_conditions=False)
    stack = contextlib.ExitStack()
    ap = lambda h: h.ap()

    x_in = ap(nc.dram_tensor("x", [NSEQ, T, D], F32, kind="ExternalInput"))
    wqkvT = ap(nc.dram_tensor("wqkvT", [D, 3 * D], F32, kind="ExternalInput"))
    battn = ap(nc.dram_tensor("battn", [128, 24], F32, kind="ExternalInput"))
    laTa = ap(nc.dram_tensor("laTa", [128, 256], F32, kind="ExternalInput"))
    lbTa = ap(nc.dram_tensor("lbTa", [32, 3 * D], F32, kind="ExternalInput"))
    wprojT = ap(nc.dram_tensor("wprojT", [D, D], F32, kind="ExternalInput"))
    bproj = ap(nc.dram_tensor("bproj", [128, 8], F32, kind="ExternalInput"))
    laTp = ap(nc.dram_tensor("laTp", [128, 256], F32, kind="ExternalInput"))
    lbTp = ap(nc.dram_tensor("lbTp", [32, D], F32, kind="ExternalInput"))
    w1 = ap(nc.dram_tensor("w1", [D, FF], F32, kind="ExternalInput"))
    b1c = ap(nc.dram_tensor("b1c", [128, 32], F32, kind="ExternalInput"))
    w2 = ap(nc.dram_tensor("w2", [FF, D], F32, kind="ExternalInput"))
    b2c = ap(nc.dram_tensor("b2c", [128, 8], F32, kind="ExternalInput"))
    identx = ap(nc.dram_tensor("identx", [128, 128], F32, kind="ExternalInput"))
    maskx = ap(nc.dram_tensor("maskx", [128, 896], BF16, kind="ExternalInput"))
    cvec = ap(nc.dram_tensor("cvec", [128, 8], F32, kind="ExternalInput"))
    out_hbm = ap(nc.dram_tensor("out", [NSEQ, T, D], F32,
                                kind="ExternalOutput"))

    wqkv_h = ap(nc.dram_tensor("wqkv_h", [24, 8, 128, 128], BF16))
    wqkv_l = ap(nc.dram_tensor("wqkv_l", [24, 8, 128, 128], BF16))
    wproj_h = ap(nc.dram_tensor("wproj_h", [8, 8, 128, 128], BF16))
    wproj_l = ap(nc.dram_tensor("wproj_l", [8, 8, 128, 128], BF16))
    w1_hs = ap(nc.dram_tensor("w1_hs", [32, 8, 128, 128], BF16))
    w1_ls = ap(nc.dram_tensor("w1_ls", [32, 8, 128, 128], BF16))
    w2_hs = ap(nc.dram_tensor("w2_hs", [8, 32, 128, 128], BF16))
    w2_ls = ap(nc.dram_tensor("w2_ls", [8, 32, 128, 128], BF16))

    sbuf = lambda nm, sh, dt=F32: nc.alloc_sbuf_tensor(nm, sh, dt).ap()
    B1 = sbuf("B1", [128, 32768], BF16)
    B2 = sbuf("B2", [128, 32768], BF16)
    B3 = sbuf("B3", [128, 14336], BF16)
    F1 = sbuf("F1", [128, 8192], F32)

    xhat_h = B1[:, 0:16384]
    xhat_l = B1[:, 16384:32768]
    x1h = B1[:, 0:16384]
    x1l = B1[:, 16384:32768]

    y_h = B2[:, 0:16384]
    y_l = B2[:, 16384:32768]
    hgel_h = B2[:, 0:16384]
    hgel_l = B2[:, 16384:32768]
    wsp_h = B2[:, 0:2048]
    wsp_l = B2[:, 2048:4096]
    u_l = B2[0:32, 14336:16384]
    u_h = B2[0:32, 30720:32768]

    qg_h = B3[:, 0:2048]
    qg_l = B3[:, 2048:4096]
    kg_h = B3[:, 4096:6144]
    kg_l = B3[:, 6144:8192]
    vt_h = B3[:, 8192:10240]
    vt_l = B3[:, 10240:12288]
    pt_h = B3[:, 12288:13312]
    pt_l = B3[:, 13312:14336]
    xh1_h = B3[:, 0:4096]
    xh1_l = B3[:, 4096:8192]
    u2_h = B3[0:32, 12288:12800]
    u2_l = B3[0:32, 12800:13312]

    wsplit = F1[:, 0:2048]
    xin = F1[:, 0:2048]
    xh_tm = F1[:, 2048:3072]
    sq_scr1 = F1[:, 3072:4096]
    sq_ln = sq_scr1
    vstage = F1[:, 4096:4608]
    ef32 = [F1[:, 4608 + i * 512: 4608 + (i + 1) * 512] for i in range(2)]
    rb64 = F1[:, 5632:6144]
    ystage = F1[:, 6144:6656]
    rrow = F1[0:1, 6656:7168]
    pstage = F1[:, 0:4096]
    x1stg = F1[:, 4096:5120]
    ustage = F1[0:32, 5120:5632]
    xres = F1[:, 5632:7680]
    GB = [F1[:, i * 512: (i + 1) * 512] for i in range(8)]
    GB2 = [F1[:, 4096 + i * 512: 4096 + (i + 1) * 512]
           for i in range(8)]
    ot_st = F1[:, 0:2048]
    xfull = F1[:, 0:1024]
    ofm = F1[:, 4096:8192]

    ws_h = sbuf("ws_h", [128, 2048], BF16)
    ws_l = sbuf("ws_l", [128, 2048], BF16)

    identt = sbuf("identt", [128, 128])
    maskt = sbuf("maskt", [128, 896], BF16)
    cv = sbuf("cv", [128, 8])
    bat = sbuf("bat", [128, 24])
    bpt = sbuf("bpt", [128, 8])
    b1t = sbuf("b1t", [128, 32])
    b2t = sbuf("b2t", [128, 8])
    la_h = sbuf("la_h", [128, 256], BF16)
    la_l = sbuf("la_l", [128, 256], BF16)
    lap_h = sbuf("lap_h", [128, 256], BF16)
    lap_l = sbuf("lap_l", [128, 256], BF16)
    lbp_h = sbuf("lbp_h", [32, D], BF16)
    lbp_l = sbuf("lbp_l", [32, D], BF16)
    lbs_h = sbuf("lbs_h", [32, 256], BF16)
    lbs_l = sbuf("lbs_l", [32, 256], BF16)
    lbstg = sbuf("lbstg", [32, 256])
    s1c = sbuf("s1c", [128, 16])
    s2c = sbuf("s2c", [128, 16])
    mcol = sbuf("mcol", [128, 16])
    vcol = sbuf("vcol", [128, 16])
    rstd = sbuf("rstd", [128, 16])
    rstd0 = sbuf("rstd0", [128, 16])
    vcol0 = sbuf("vcol0", [128, 16])
    mbias = sbuf("mbias", [128, 16])
    ones_bf = sbuf("ones_bf", [128, 1], BF16)
    ones_row = sbuf("ones_row", [1, 128], F32)
    fcol = sbuf("fcol", [128, 4], F32)

    pball = lambda nm: nc.alloc_psum_tensor(nm, [128, 512], F32).ap()
    pA = [pball("pA0"), pball("pA1")]
    pT = [pball("pT0"), pball("pT1")]
    pS = [pball("pS0"), pball("pS1")]
    pY = [pball("pY0"), pball("pY1")]

    P = Prog(nc)
    P.make_sems(stack)
    CV_Y1, CV_EXP, CV_EPS, CV_NT = 0, 1, 2, 3
    DBG = bool(os.environ.get("KDBG"))
    PHASES = int(os.environ.get("KPHASE", "99"))

    def dbgt(name, src_ap, toks):
        if not DBG:
            return
        t = nc.dram_tensor("dbg_" + name, list(src_ap.shape), src_ap.dtype,
                           kind="ExternalOutput").ap()
        P.waits("sync", list(toks))
        P.dma("sync", t, src_ap)

    x_r = x_in.rearrange("s (n p) d -> s n p d", p=128)
    out_r = out_hbm.rearrange("s (n p) d -> s n p d", p=128)

    tok_consts = [P.dma("sync", identt, identx), P.dma("sync", maskt, maskx),
                  P.dma("sync", cv, cvec), P.dma("sync", bat, battn),
                  P.dma("sync", bpt, bproj)]
    tok_b = [P.dma("sync", b1t, b1c), P.dma("sync", b2t, b2c)]
    P.waits("vector", tok_b)
    P.v(lambda e: e.tensor_scalar(out=b1t, in0=b1t, scalar1=256.0,
                                  scalar2=None, op0=ALU.mult))
    P.v(lambda e: e.tensor_scalar(out=b2t, in0=b2t, scalar1=256.0,
                                  scalar2=None, op0=ALU.mult))
    P.v(lambda e: e.memset(ones_bf, 1.0))
    P.v(lambda e: e.memset(ones_row, 1.0))
    P.v(lambda e: e.memset(fcol[:, 0:1], 1.0))
    P.drain("vector")
    P.v(lambda e: e.tensor_scalar(out=fcol[:, 1:2], in0=fcol[:, 0:1],
                                  scalar1=float(2.0**-40), scalar2=1.0,
                                  op0=ALU.add, op1=ALU.subtract))
    P.drain("vector")
    P.v(lambda e: e.tensor_scalar(out=fcol[:, 2:3], in0=fcol[:, 1:2],
                                  scalar1=float(2.0**40), scalar2=None,
                                  op0=ALU.mult))
    P.drain("vector")
    P.v(lambda e: e.tensor_scalar(out=fcol[:, 3:4], in0=fcol[:, 2:3],
                                  scalar1=-1.0, scalar2=None, op0=ALU.mult))
    P.drain("vector")
    P.flag_col = fcol[:, 2:3]
    P.nflag_col = fcol[:, 3:4]

    prev_split = [None]

    def split_to_hbm(src, hi, lo, rows, cols):
        for r0 in range(0, rows, 128):
            ci = r0 // 128
            c = 0
            while c < cols:
                w = min(2048, cols - c)
                nb = w // 128
                o0 = c // 128
                t_in = P.dma("sync", wsplit[:, 0:w], src[r0:r0 + 128, c:c + w])
                P.waits("vector", [t_in, prev_split[0]])
                P.waits("scalar", [t_in, prev_split[0]])
                P.split(wsplit[:, 0:w], wsp_h[:, 0:w], wsp_l[:, 0:w])
                mv = P.mark("vector")
                P.wait("sync", mv)
                srch = wsp_h[:, 0:w].rearrange("p (o x) -> p o x", o=nb)
                srcl = wsp_l[:, 0:w].rearrange("p (o x) -> p o x", o=nb)
                dsth = hi[o0:o0 + nb, ci].rearrange("o p x -> p o x")
                dstl = lo[o0:o0 + nb, ci].rearrange("o p x -> p o x")
                P.dma("sync", dsth, srch)
                prev_split[0] = P.dma("sync", dstl, srcl)
                c += w

    split_to_hbm(wqkvT, wqkv_h, wqkv_l, D, 3 * D)
    split_to_hbm(wprojT, wproj_h, wproj_l, D, D)
    split_to_hbm(w1, w1_hs, w1_ls, D, FF)
    split_to_hbm(w2, w2_hs, w2_ls, FF, D)

    def load_split_small(src, hi, lo, prow, cols):
        t_in = P.dma("sync", wsplit[0:prow, 0:cols], src)
        P.waits("scalar", [t_in, prev_split[0]])
        P.waits("vector", [t_in, prev_split[0]])
        P.split(wsplit[0:prow, 0:cols], hi, lo)
        mv = P.mark("vector")
        P.wait("sync", mv)
        prev_split[0] = mv
        return mv

    load_split_small(laTa, la_h, la_l, 128, 256)
    load_split_small(laTp, lap_h, lap_l, 128, 256)
    tok_small = load_split_small(lbTp, lbp_h, lbp_l, 32, D)
    P.waits("tensor", tok_consts + [tok_small])
    P.waits("scalar", tok_consts)
    P.waits("vector", tok_consts)
    P.waits("gpsimd", tok_consts)

    ws_state = {"idx": 0, "guard": [None, None]}

    def ws_load(srcs):
        h = ws_state["idx"] % 2
        ws_state["idx"] += 1
        P.wait("sync", ws_state["guard"][h])
        P.wait("gpsimd", ws_state["guard"][h])
        off = h * 1024
        tks = []
        pos = 0
        for cols, sh, sl, nch in srcs:
            dsth = ws_h[:, off + pos: off + pos + cols].rearrange(
                "p (c o) -> p c o", c=nch)
            dstl = ws_l[:, off + pos: off + pos + cols].rearrange(
                "p (c o) -> p c o", c=nch)
            tks.append(P.dma("sync", dsth, sh.rearrange("c p o -> p c o")))
            tks.append(P.dma("sync", dstl, sl.rearrange("c p o -> p c o")))
            pos += cols
        return h, tks

    def ws_done(h, tok):
        ws_state["guard"][h] = tok

    lb_state = {"idx": 0, "guard": [None, None]}

    def lb_load(col):
        h = lb_state["idx"] % 2
        lb_state["idx"] += 1
        P.wait("sync", lb_state["guard"][h])
        tk = P.dma("sync", lbstg[:, h * 128:(h + 1) * 128],
                   lbTa[:, col:col + 128])
        P.waits("scalar", [tk])
        P.waits("vector", [tk])
        P.split(lbstg[:, h * 128:(h + 1) * 128],
                lbs_h[:, h * 128:(h + 1) * 128],
                lbs_l[:, h * 128:(h + 1) * 128])
        return h, P.mark("vector")

    def lb_done(h, tok):
        lb_state["guard"][h] = tok

    def wslice(hr, lr, c0, c1, oblk):
        return [((c1 - c0) * 128, hr[oblk, c0:c1], lr[oblk, c0:c1], c1 - c0)]

    tok_seq_end = None
    tok_out_prev = None

    tok_p0_out = prev_split[0]
    for s in range(NSEQ):
        P.wait("scalar", tok_p0_out)
        P.wait("vector", tok_p0_out)
        P.wait("gpsimd", tok_p0_out)
        norm_done = [None, None]
        ptev = [None, None]
        for t in range(NTILE):
            sl = (t % 2) * 1024
            P.wait("gpsimd", norm_done[t % 2])
            if s > 0:
                P.wait("gpsimd", tok_seq_end)
                P.wait("gpsimd", tok_out_prev)
            tok_x = P.dma("gpsimd", xin[:, sl:sl + 1024], x_r[s, t])
            if s == 0 and t == 0:
                x_toks_dbg = tok_x
            P.wait("scalar", tok_x)
            P.wait("vector", tok_x)
            if s > 0:
                P.waits("scalar", [tok_out_prev])
                P.waits("vector", [tok_out_prev])
            P.s(lambda e, sl=sl, t=t: e.activation(
                out=sq_scr1, in_=xin[:, sl:sl + 1024],
                func=AF.Square, accum_out=s2c[:, t:t + 1]))
            P.v(lambda e, sl=sl, t=t: e.reduce_sum(
                out=s1c[:, t:t + 1], in_=xin[:, sl:sl + 1024], axis=AX))
            ms = P.mark("scalar")
            P.wait("vector", ms)
            P.drain("vector")
            P.v(lambda e, t=t: e.tensor_scalar(
                out=mcol[:, t:t + 1], in0=s1c[:, t:t + 1], scalar1=1.0 / D,
                scalar2=None, op0=ALU.mult))
            P.drain("vector")
            P.v(lambda e, t=t: e.tensor_tensor(
                out=vcol0[:, t:t + 1], in0=mcol[:, t:t + 1],
                in1=mcol[:, t:t + 1], op=ALU.mult))
            P.drain("vector")
            P.v(lambda e, t=t: e.scalar_tensor_tensor(
                out=vcol[:, t:t + 1], in0=s2c[:, t:t + 1], scalar=1.0 / D,
                in1=vcol0[:, t:t + 1], op0=ALU.mult, op1=ALU.subtract))
            mv = P.mark("vector")
            P.wait("scalar", mv)
            P.s(lambda e, t=t: e.activation(
                out=rstd0[:, t:t + 1], in_=vcol[:, t:t + 1], func=AF.Sqrt,
                bias=cv[:, CV_EPS:CV_EPS + 1]))
            ms2 = P.mark("scalar")
            P.wait("vector", ms2)
            P.v(lambda e, t=t: e.reciprocal(out=rstd[:, t:t + 1],
                                            in_=rstd0[:, t:t + 1]))
            P.drain("vector")
            P.v(lambda e, t=t: e.scalar_tensor_tensor(
                out=mbias[:, t:t + 1], in0=mcol[:, t:t + 1], scalar=-1.0,
                in1=rstd[:, t:t + 1], op0=ALU.mult, op1=ALU.mult))
            mv2 = P.mark("vector")
            P.wait("scalar", mv2)
            P.s(lambda e, sl=sl, t=t: e.activation(
                out=xh_tm, in_=xin[:, sl:sl + 1024], func=AF.Identity,
                scale=rstd[:, t:t + 1], bias=mbias[:, t:t + 1]))
            norm_done[t % 2] = P.mark("scalar")
            if s == 0 and t == 0:
                dbgt("xin0", xin[:, 0:1024], [x_toks_dbg])
                dbgt("stats", s1c, [norm_done[0]])
                dbgt("stats2", s2c, [norm_done[0]])
                dbgt("rstd", rstd, [norm_done[0]])
                dbgt("mcol", mcol, [norm_done[0]])
                dbgt("vcol", vcol, [norm_done[0]])
                dbgt("rstd0", rstd0, [norm_done[0]])
                dbgt("xhtm0", xh_tm, [norm_done[0]])
            P.wait("tensor", norm_done[t % 2])
            for half in range(2):
                P.wait("tensor", ptev[half])
                for j in range(4):
                    c = half * 4 + j
                    P.t(lambda e, half=half, j=j, c=c: e.transpose(
                        pT[half][:, j * 128:(j + 1) * 128],
                        xh_tm[:, c * 128:(c + 1) * 128], identt))
            mt = P.mark("tensor")
            P.wait("scalar", mt)
            P.wait("vector", mt)
            for half in range(2):
                c0 = half * 4
                dsth = xhat_h.rearrange("p (c w) -> p c w", c=8)[
                    :, c0:c0 + 4, t * 128:(t + 1) * 128]
                dstl = xhat_l.rearrange("p (c w) -> p c w", c=8)[
                    :, c0:c0 + 4, t * 128:(t + 1) * 128]
                ptr = pT[half].rearrange("p (c w) -> p c w", c=4)
                P.split(ptr, dsth, dstl)
                ptev[half] = P.mark("vector")
        tok_xhat = P.mark("vector")
        if s == 0:
            dbgt("xhat_h", xhat_h[:, 0:2048], [tok_xhat, P.mark("scalar")])
            dbgt("xhat_l", xhat_l[:, 0:2048], [tok_xhat])
        if PHASES <= 1:
            break

        P.waits("tensor", [tok_xhat])
        ev_prev = [None, None]
        for tg in range(NTG):
            ab = tg % 2
            P.wait("tensor", ev_prev[ab])

            def u_mms(e, tg=tg, ab=ab):
                last = None
                for c in range(8):
                    wh = la_h[:, c * 32:(c + 1) * 32]
                    wl = la_l[:, c * 32:(c + 1) * 32]
                    xh = xhat_h[:, c * 2048 + tg * 512:
                                c * 2048 + (tg + 1) * 512]
                    xl = xhat_l[:, c * 2048 + tg * 512:
                                c * 2048 + (tg + 1) * 512]
                    e.matmul(pA[ab][0:32, :], wh, xh, start=(c == 0),
                             stop=False)
                    e.matmul(pA[ab][0:32, :], wh, xl, start=False, stop=False)
                    last = e.matmul(pA[ab][0:32, :], wl, xh, start=False,
                                    stop=(c == 7))
                return last
            mm_tok = P.mark_inst("tensor", u_mms)
            P.wait("scalar", mm_tok)
            P.s(lambda e, ab=ab: e.activation(
                out=ustage, in_=pA[ab][0:32, :], func=AF.Identity, scale=0.5))
            ms = P.mark("scalar")
            P.wait("vector", ms)
            P.v(lambda e, tg=tg: e.tensor_copy(
                out=u_h[:, tg * 512:(tg + 1) * 512], in_=ustage))
            P.v(lambda e, tg=tg: e.scalar_tensor_tensor(
                out=u_l[:, tg * 512:(tg + 1) * 512], in0=ustage, scalar=1.0,
                in1=u_h[:, tg * 512:(tg + 1) * 512], op0=ALU.mult,
                op1=ALU.subtract))
            ev_prev[ab] = P.mark("vector")
        tok_u = P.mark("vector")
        if s == 0:
            dbgt("u_h", u_h, [tok_u])
            dbgt("u_l", u_l, [tok_u])
        if PHASES <= 2:
            break

        pair_exp = None
        pv_prev = None
        sp_prev = None
        ynorm_done = [None, None]
        for g in range(NHG):
            P.waits("tensor", [tok_xhat, tok_u])
            P.wait("tensor", ynorm_done[0])
            P.wait("tensor", ynorm_done[1])
            qkv_names = [(0, qg_h, qg_l), (1, kg_h, kg_l), (2, None, None)]
            for pi in range(3):
                wcol = pi * D + g * 128
                wh_half, wtok = ws_load(
                    wslice(wqkv_h, wqkv_l, 0, 8, wcol // 128))
                lb_half, lbtok = lb_load(wcol)
                P.waits("tensor", wtok + [lbtok])
                last_mm = None
                for tg in range(NTG):
                    ab = tg % 2
                    P.wait("tensor", ev_prev[ab])

                    def qkv_mms(e, wh_half=wh_half, lb_half=lb_half, tg=tg,
                                ab=ab):
                        off = wh_half * 1024
                        for c in range(8):
                            wh = ws_h[:, off + c * 128: off + (c + 1) * 128]
                            wl = ws_l[:, off + c * 128: off + (c + 1) * 128]
                            xh = xhat_h[:, c * 2048 + tg * 512:
                                        c * 2048 + (tg + 1) * 512]
                            xl = xhat_l[:, c * 2048 + tg * 512:
                                        c * 2048 + (tg + 1) * 512]
                            e.matmul(pA[ab], wh, xh, start=(c == 0),
                                     stop=False)
                            e.matmul(pA[ab], wh, xl, start=False, stop=False)
                            e.matmul(pA[ab], wl, xh, start=False, stop=False)
                        lo = lb_half * 128
                        lh = lbs_h[:, lo:lo + 128]
                        ll = lbs_l[:, lo:lo + 128]
                        uh = u_h[:, tg * 512:(tg + 1) * 512]
                        ul = u_l[:, tg * 512:(tg + 1) * 512]
                        e.matmul(pA[ab], lh, uh, start=False, stop=False)
                        e.matmul(pA[ab], lh, ul, start=False, stop=False)
                        return e.matmul(pA[ab], ll, uh, start=False, stop=True)
                    mm_tok = P.mark_inst("tensor", qkv_mms)
                    last_mm = mm_tok
                    bias_col = bat[:, pi * 8 + g: pi * 8 + g + 1]
                    if pi < 2:
                        dh = qkv_names[pi][1][:, tg * 512:(tg + 1) * 512]
                        dl = qkv_names[pi][2][:, tg * 512:(tg + 1) * 512]
                        P.wait("scalar", mm_tok)

                        def hi_ev(e, dh=dh, ab=ab, bias_col=bias_col):
                            e.activation(out=dh, in_=pA[ab], func=AF.Identity,
                                         bias=bias_col)
                        P.s(hi_ev)
                        ms = P.mark("scalar")
                        P.wait("vector", ms)
                        P.v(lambda e, dl=dl, ab=ab, bias_col=bias_col, dh=dh:
                            e.scalar_tensor_tensor(
                                out=dl, in0=pA[ab], scalar=bias_col, in1=dh,
                                op0=ALU.add, op1=ALU.subtract))
                        ev_prev[ab] = P.mark("vector")
                    else:
                        P.wait("scalar", mm_tok)
                        P.s(lambda e, ab=ab, bias_col=bias_col: e.activation(
                            out=vstage, in_=pA[ab], func=AF.Identity,
                            bias=bias_col))
                        ms = P.mark("scalar")
                        ev_prev[ab] = ms
                        P.wait("tensor", ms)
                        ph = pT[tg % 2]
                        P.wait("tensor", ptev[tg % 2])
                        for j in range(4):
                            P.t(lambda e, ph=ph, j=j: e.transpose(
                                ph[:, j * 128:(j + 1) * 128],
                                vstage[:, j * 128:(j + 1) * 128], identt))
                        mt = P.mark("tensor")
                        P.wait("scalar", mt)
                        P.wait("vector", mt)
                        dsth = vt_h[:, tg * 512:(tg + 1) * 512]
                        dstl = vt_l[:, tg * 512:(tg + 1) * 512]
                        P.split(ph, dsth, dstl)
                        ptev[tg % 2] = P.mark("vector")
                ws_done(wh_half, last_mm)
                lb_done(lb_half, last_mm)
            tok_qkv = [P.mark("scalar"), P.mark("vector")]
            if s == 0 and g == 0:
                dbgt("qg_h", qg_h, tok_qkv)
                dbgt("qg_l", qg_l, tok_qkv)
                dbgt("kg_h", kg_h, tok_qkv)
                dbgt("vt_h", vt_h, tok_qkv)
                dbgt("vt_l", vt_l, tok_qkv)

            P.waits("tensor", tok_qkv)
            for h2 in range(2):
                hp = 64 * h2
                for qb in range(NTG):
                    nkt = 4 * (qb + 1)
                    npair = nkt // 2
                    yi = (h2 * NTG + qb) % 2
                    P.wait("tensor", ynorm_done[yi])

                    def emit_pv(jp, sp_tok, yi=yi, hp=hp, nkt=nkt):
                        P.wait("tensor", sp_tok)
                        tp = (0, hp) if hp else None

                        def pvmms(e, hp=hp, jp=jp, yi=yi, nkt=nkt, tp=tp):
                            last = None
                            for u in range(2):
                                j = 2 * jp + u
                                po = u * 512
                                vh = vt_h[:, j * 128 + hp: j * 128 + hp + 64]
                                vl = vt_l[:, j * 128 + hp: j * 128 + hp + 64]
                                ph_ = pt_h[:, po:po + 512]
                                pl_ = pt_l[:, po:po + 512]
                                st = (j == 0)
                                yps = pY[yi][hp:hp + 64, :]
                                sps = pA[yi][0:1, :]
                                e.matmul(yps, vh, ph_, start=st, stop=False,
                                         tile_position=tp)
                                e.matmul(yps, vh, pl_, start=False,
                                         stop=False, tile_position=tp)
                                e.matmul(yps, vl, ph_, start=False,
                                         stop=(j == nkt - 1),
                                         tile_position=tp)
                                e.matmul(sps, ones_bf, ph_, start=st,
                                         stop=False)
                                last = e.matmul(sps, ones_bf, pl_,
                                                start=False,
                                                stop=(j == nkt - 1))
                            return last
                        return P.mark_inst("tensor", pvmms)

                    for jp in range(npair):
                        kt0 = 2 * jp
                        P.wait("tensor", pair_exp)

                        def smms(e, hp=hp, kt0=kt0, qb=qb):
                            last = None
                            for u in range(2):
                                kt = kt0 + u
                                kh = kg_h[hp:hp + 64,
                                          kt * 128:(kt + 1) * 128]
                                kl = kg_l[hp:hp + 64,
                                          kt * 128:(kt + 1) * 128]
                                qh = qg_h[hp:hp + 64,
                                          qb * 512:(qb + 1) * 512]
                                ql = qg_l[hp:hp + 64,
                                          qb * 512:(qb + 1) * 512]
                                e.matmul(pS[u], kh, qh, start=True,
                                         stop=False)
                                e.matmul(pS[u], kh, ql, start=False,
                                         stop=False)
                                last = e.matmul(pS[u], kl, qh, start=False,
                                                stop=True)
                            return last
                        mm_tok = P.mark_inst("tensor", smms)
                        if jp >= 1:
                            pv_prev = emit_pv(jp - 1, sp_prev)
                        pre_exp = mm_tok
                        if kt0 >= 4 * qb:
                            P.wait("vector", mm_tok)
                            for u in range(2):
                                r = kt0 + u - 4 * qb
                                mc0 = 384 - 128 * r
                                P.v(lambda e, u=u, mc0=mc0: e.tensor_tensor(
                                    out=pS[u], in0=pS[u],
                                    in1=maskt[:, mc0:mc0 + 512], op=ALU.add))
                            pre_exp = P.mark("vector")
                        P.wait("scalar", pre_exp)
                        P.wait("scalar", sp_prev)
                        for u in range(2):
                            eb = ef32[u]
                            P.s(lambda e, u=u, eb=eb: e.activation(
                                out=eb, in_=pS[u], func=AF.Exp, scale=0.125,
                                bias=cv[:, CV_EXP:CV_EXP + 1]))
                        pair_exp = P.mark("scalar")
                        P.wait("vector", pair_exp)
                        P.wait("vector", pv_prev)
                        for u in range(2):
                            eb = ef32[u]
                            po = u * 512
                            P.v(lambda e, eb=eb, po=po: e.tensor_copy(
                                out=pt_h[:, po:po + 512], in_=eb))
                        for u in range(2):
                            eb = ef32[u]
                            po = u * 512
                            P.v(lambda e, eb=eb, po=po:
                                e.scalar_tensor_tensor(
                                    out=pt_l[:, po:po + 512], in0=eb,
                                    scalar=1.0, in1=pt_h[:, po:po + 512],
                                    op0=ALU.mult, op1=ALU.subtract))
                        sp_prev = P.mark("vector")
                    pv_prev = emit_pv(npair - 1, sp_prev)

                    P.wait("scalar", pv_prev)
                    P.s(lambda e, yi=yi: e.activation(
                        out=rrow, in_=pA[yi][0:1, :], func=AF.Identity))
                    msr = P.mark("scalar")
                    P.wait("tensor", msr)
                    P.wait("tensor", ptev[0])
                    bc_tok = P.mark_inst("tensor", lambda e: e.matmul(
                        pT[0], ones_row, rrow, start=True, stop=True))
                    P.wait("scalar", bc_tok)
                    P.s(lambda e: e.activation(out=rb64, in_=pT[0],
                                               func=AF.Identity))
                    mbc = P.mark("scalar")
                    P.wait("vector", mbc)
                    P.v(lambda e: e.reciprocal(out=rb64, in_=rb64))
                    P.v(lambda e, yi=yi, hp=hp: e.tensor_tensor(
                        out=ystage[hp:hp + 64, :], in0=pY[yi][hp:hp + 64, :],
                        in1=rb64[hp:hp + 64, :], op=ALU.mult))
                    yh = y_h[hp:hp + 64, g * 2048 + qb * 512:
                             g * 2048 + (qb + 1) * 512]
                    yl = y_l[hp:hp + 64, g * 2048 + qb * 512:
                             g * 2048 + (qb + 1) * 512]
                    P.v(lambda e, yh=yh, hp=hp: e.tensor_copy(
                        out=yh, in_=ystage[hp:hp + 64, :]))
                    P.v(lambda e, yh=yh, yl=yl, hp=hp: e.scalar_tensor_tensor(
                        out=yl, in0=ystage[hp:hp + 64, :], scalar=1.0, in1=yh,
                        op0=ALU.mult, op1=ALU.subtract))
                    ynorm_done[yi] = P.mark("vector")
                    ptev[0] = ynorm_done[yi]

        tok_y = P.mark("vector")
        if s == 0:
            dbgt("y_h", y_h[:, 0:4096], [tok_y])
            dbgt("y_l", y_l[:, 0:4096], [tok_y])
        if PHASES <= 3:
            break

        out_dma_toks = [None, None]
        pstage_guard = None
        xg = [None, None]
        P.wait("gpsimd", tok_y)
        for tg in range(NTG):
            P.waits("tensor", [tok_y])

            def u2_mms(e, tg=tg):
                last = None
                for c in range(8):
                    wh = lap_h[:, c * 32:(c + 1) * 32]
                    wl = lap_l[:, c * 32:(c + 1) * 32]
                    yh = y_h[:, c * 2048 + tg * 512: c * 2048 + (tg + 1) * 512]
                    yl = y_l[:, c * 2048 + tg * 512: c * 2048 + (tg + 1) * 512]
                    e.matmul(pA[0][0:32, :], wh, yh, start=(c == 0),
                             stop=False)
                    e.matmul(pA[0][0:32, :], wh, yl, start=False, stop=False)
                    last = e.matmul(pA[0][0:32, :], wl, yh, start=False,
                                    stop=(c == 7))
                return last
            mm_tok = P.mark_inst("tensor", u2_mms)
            P.wait("scalar", mm_tok)
            P.s(lambda e: e.activation(out=ustage, in_=pA[0][0:32, :],
                                       func=AF.Identity, scale=0.5))
            ms = P.mark("scalar")
            P.wait("vector", ms)
            P.v(lambda e: e.tensor_copy(out=u2_h, in_=ustage))
            P.v(lambda e: e.scalar_tensor_tensor(
                out=u2_l, in0=ustage, scalar=1.0, in1=u2_h, op0=ALU.mult,
                op1=ALU.subtract))
            mu2 = P.mark("vector")
            P.wait("tensor", mu2)

            pr_ev = [None, None]
            for oc in range(8):
                ab = oc % 2
                whalf, wtok = ws_load(
                    wslice(wproj_h, wproj_l, 0, 8, oc))
                P.waits("tensor", wtok)
                P.wait("tensor", pr_ev[ab])

                def proj_all(e, whalf=whalf, tg=tg, ab=ab, oc=oc):
                    off = whalf * 1024
                    for c in range(8):
                        wh = ws_h[:, off + c * 128: off + (c + 1) * 128]
                        wl = ws_l[:, off + c * 128: off + (c + 1) * 128]
                        yh = y_h[:, c * 2048 + tg * 512:
                                 c * 2048 + (tg + 1) * 512]
                        yl = y_l[:, c * 2048 + tg * 512:
                                 c * 2048 + (tg + 1) * 512]
                        e.matmul(pA[ab], wh, yh, start=(c == 0), stop=False)
                        e.matmul(pA[ab], wh, yl, start=False, stop=False)
                        e.matmul(pA[ab], wl, yh, start=False, stop=False)
                    lh = lbp_h[:, oc * 128:(oc + 1) * 128]
                    ll = lbp_l[:, oc * 128:(oc + 1) * 128]
                    e.matmul(pA[ab], lh, u2_h, start=False, stop=False)
                    e.matmul(pA[ab], lh, u2_l, start=False, stop=False)
                    return e.matmul(pA[ab], ll, u2_h, start=False, stop=True)
                mm_tok = P.mark_inst("tensor", proj_all)
                ws_done(whalf, mm_tok)
                P.wait("scalar", mm_tok)
                P.wait("scalar", pstage_guard)
                P.s(lambda e, ab=ab, oc=oc: e.activation(
                    out=pstage[:, oc * 512:(oc + 1) * 512], in_=pA[ab],
                    func=AF.Identity, bias=bpt[:, oc:oc + 1]))
                pr_ev[ab] = P.mark("scalar")
            tok_proj = P.mark("scalar")
            if s == 0 and tg == 0:
                dbgt("pstage", pstage, [tok_proj])
                dbgt("u2_h", u2_h, [tok_proj])

            P.waits("tensor", [tok_proj])
            for j in range(4):
                P.wait("gpsimd", xg[j % 2])
                tok_xr = P.dma("gpsimd", xres[:, (j % 2) * 1024:
                                               (j % 2) * 1024 + 1024],
                               x_r[s, tg * 4 + j])
                for half in range(2):
                    P.wait("tensor", ptev[half])
                    for jo in range(4):
                        oc = half * 4 + jo
                        P.t(lambda e, half=half, jo=jo, oc=oc, j=j:
                            e.transpose(pT[half][:, jo * 128:(jo + 1) * 128],
                                        pstage[:, oc * 512 + j * 128:
                                               oc * 512 + (j + 1) * 128],
                                        identt))
                mt = P.mark("tensor")
                P.wait("vector", mt)
                P.wait("vector", tok_xr)
                for half in range(2):
                    P.v(lambda e, half=half, j=j: e.tensor_tensor(
                        out=x1stg[:, half * 512:(half + 1) * 512],
                        in0=pT[half],
                        in1=xres[:, (j % 2) * 1024 + half * 512:
                                 (j % 2) * 1024 + (half + 1) * 512],
                        op=ALU.add))
                mv0 = P.mark("vector")
                P.wait("scalar", mv0)
                tix = (tg * 4 + j) * 1024
                P.split(x1stg, x1h[:, tix:tix + 1024], x1l[:, tix:tix + 1024])
                mvs = P.mark("vector")
                ptev[0] = mvs
                ptev[1] = mvs
                xg[j % 2] = mvs
            pstage_guard = P.mark("tensor")
        tok_x1all = [P.mark("scalar"), P.mark("vector")]
        if s == 0:
            dbgt("x1h", x1h[:, 0:4096], tok_x1all)
            dbgt("x1l", x1l[:, 0:4096], tok_x1all)

        ffn_done = None
        for tg in range(NTG):
            P.waits("scalar", tok_x1all)
            P.waits("vector", tok_x1all)
            for j in range(4):
                tix = (tg * 4 + j) * 1024
                if j == 0:
                    P.waits("vector", out_dma_toks)
                P.v(lambda e, tix=tix: e.tensor_tensor(
                    out=xfull, in0=x1h[:, tix:tix + 1024],
                    in1=x1l[:, tix:tix + 1024], op=ALU.add))
                mvx = P.mark("vector")
                P.wait("scalar", mvx)
                P.s(lambda e, j=j: e.activation(
                    out=sq_ln, in_=xfull, func=AF.Square,
                    accum_out=s2c[:, j:j + 1]))
                P.v(lambda e, j=j: e.reduce_sum(
                    out=s1c[:, j:j + 1], in_=xfull, axis=AX))
                msq = P.mark("scalar")
                P.wait("vector", msq)
                P.drain("vector")
                P.v(lambda e, j=j: e.tensor_scalar(
                    out=mcol[:, j:j + 1], in0=s1c[:, j:j + 1],
                    scalar1=1.0 / D, scalar2=None, op0=ALU.mult))
                P.drain("vector")
                P.v(lambda e, j=j: e.tensor_tensor(
                    out=vcol0[:, j:j + 1], in0=mcol[:, j:j + 1],
                    in1=mcol[:, j:j + 1], op=ALU.mult))
                P.drain("vector")
                P.v(lambda e, j=j: e.scalar_tensor_tensor(
                    out=vcol[:, j:j + 1], in0=s2c[:, j:j + 1],
                    scalar=1.0 / D, in1=vcol0[:, j:j + 1], op0=ALU.mult,
                    op1=ALU.subtract))
                mvc = P.mark("vector")
                P.wait("scalar", mvc)
                P.s(lambda e, j=j: e.activation(
                    out=rstd0[:, j:j + 1], in_=vcol[:, j:j + 1], func=AF.Sqrt,
                    bias=cv[:, CV_EPS:CV_EPS + 1]))
                msc2 = P.mark("scalar")
                P.wait("vector", msc2)
                P.v(lambda e, j=j: e.reciprocal(out=rstd[:, j:j + 1],
                                                in_=rstd0[:, j:j + 1]))
                P.drain("vector")
                P.v(lambda e, j=j: e.scalar_tensor_tensor(
                    out=mbias[:, j:j + 1], in0=mcol[:, j:j + 1], scalar=-1.0,
                    in1=rstd[:, j:j + 1], op0=ALU.mult, op1=ALU.mult))
                mvc2 = P.mark("vector")
                P.wait("scalar", mvc2)
                P.s(lambda e, j=j: e.activation(
                    out=xh_tm, in_=xfull, func=AF.Identity,
                    scale=rstd[:, j:j + 1], bias=mbias[:, j:j + 1]))
                msx = P.mark("scalar")
                P.wait("tensor", msx)
                for half in range(2):
                    P.wait("tensor", ptev[half])
                    for jo in range(4):
                        c = half * 4 + jo
                        P.t(lambda e, half=half, jo=jo, c=c: e.transpose(
                            pT[half][:, jo * 128:(jo + 1) * 128],
                            xh_tm[:, c * 128:(c + 1) * 128], identt))
                mt = P.mark("tensor")
                P.wait("scalar", mt)
                P.wait("vector", mt)
                for half in range(2):
                    c0 = half * 4
                    dsth = xh1_h.rearrange("p (c w) -> p c w", c=8)[
                        :, c0:c0 + 4, j * 128:(j + 1) * 128]
                    dstl = xh1_l.rearrange("p (c w) -> p c w", c=8)[
                        :, c0:c0 + 4, j * 128:(j + 1) * 128]
                    ptr = pT[half].rearrange("p (c w) -> p c w", c=4)
                    P.split(ptr, dsth, dstl)
                    ptev[half] = P.mark("vector")
            tok_xh1 = [P.mark("scalar"), P.mark("vector")]

            P.waits("tensor", tok_xh1)
            gel_ev = None
            z_done = [None, None]
            hbank = [pA[0], pA[1], pS[0], pS[1]]
            SETS = [GB, GB2]
            ZS = [GB[6], GB[7]]
            TS = [GB2[6], GB2[7]]
            for fp in range(16):
                f0 = 2 * fp
                bp = fp % 2
                P.wait("tensor", z_done[bp])
                mm_tok = None
                for u in range(2):
                    f = f0 + u
                    whalf, wtok = ws_load(wslice(w1_hs, w1_ls, 0, 8, f))
                    P.waits("tensor", wtok)
                    pb = hbank[bp * 2 + u]

                    def h_mms(e, whalf=whalf, pb=pb):
                        off = whalf * 1024
                        last = None
                        for c in range(8):
                            wh = ws_h[:, off + c * 128: off + (c + 1) * 128]
                            wl = ws_l[:, off + c * 128: off + (c + 1) * 128]
                            xh = xh1_h[:, c * 512:(c + 1) * 512]
                            xl = xh1_l[:, c * 512:(c + 1) * 512]
                            e.matmul(pb, wh, xh, start=(c == 0), stop=False)
                            e.matmul(pb, wh, xl, start=False, stop=False)
                            last = e.matmul(pb, wl, xh, start=False,
                                            stop=(c == 7))
                        return last
                    mm_tok = P.mark_inst("tensor", h_mms)
                    ws_done(whalf, mm_tok)
                P.wait("scalar", mm_tok)
                P.wait("scalar", gel_ev)
                for u in range(2):
                    pb = hbank[bp * 2 + u]
                    P.s(lambda e, pb=pb, u=u, f0=f0: e.activation(
                        out=ZS[u], in_=pb, func=AF.Identity,
                        scale=256.0, bias=b1t[:, f0 + u:f0 + u + 1]))
                if fp >= 1:
                    for u in range(2):
                        fprev = f0 - 2 + u
                        ghp = hgel_h[:, fprev * 512:(fprev + 1) * 512]
                        P.s(lambda e, ghp=ghp, W=SETS[u][2]: e.activation(
                            out=ghp, in_=W, func=AF.Identity,
                            scale=1.0 / 256.0))
                z_tok = P.mark("scalar")
                z_done[bp] = z_tok
                P.wait("vector", z_tok)
                if fp == 0:
                    P.waits("vector", out_dma_toks)
                if fp >= 1:
                    for u in range(2):
                        fprev = f0 - 2 + u
                        ghp = hgel_h[:, fprev * 512:(fprev + 1) * 512]
                        glp = hgel_l[:, fprev * 512:(fprev + 1) * 512]
                        P.v(lambda e, ghp=ghp, glp=glp, W=SETS[u][2]:
                            e.scalar_tensor_tensor(
                                out=glp, in0=W, scalar=1.0 / 256.0,
                                in1=ghp, op0=ALU.mult, op1=ALU.subtract))
                for u in range(2):
                    S = SETS[u]
                    P.v(lambda e, S=S, zb=ZS[u]: e.tensor_scalar(
                        out=S[0], in0=zb, scalar1=-1024.0, scalar2=1024.0,
                        op0=ALU.max, op1=ALU.min))
                    P.rnd(S[3], S[0], S[1], S[2])
                    P.v(lambda e, S=S: e.tensor_scalar(
                        out=S[1], in0=S[0], scalar1=31.5,
                        scalar2=1.0 / 64.0, op0=ALU.subtract, op1=ALU.mult))
                    P.rnd(S[4], S[1], S[2], S[0])
                    P.v(lambda e, S=S: e.scalar_tensor_tensor(
                        out=S[2], in0=S[4], scalar=-64.0, in1=S[3],
                        op0=ALU.mult, op1=ALU.add))
                mv = P.mark("vector")
                P.wait("scalar", mv)
                for u in range(2):
                    S = SETS[u]
                    P.s(lambda e, S=S: e.activation(
                        out=S[1], in_=S[4], func=AF.Gelu, scale=0.25))
                    P.s(lambda e, S=S: e.activation(
                        out=S[0], in_=S[4], func=AF.Gelu, scale=0.25,
                        bias=cv[:, CV_Y1:CV_Y1 + 1]))
                    P.s(lambda e, zb=ZS[u], u=u: e.activation(
                        out=TS[u], in_=zb, func=AF.Relu,
                        bias=cv[:, CV_NT:CV_NT + 1]))
                msg = P.mark("scalar")
                P.wait("vector", msg)
                for u in range(2):
                    S = SETS[u]
                    P.v(lambda e, S=S: e.tensor_tensor(
                        out=S[5], in0=S[0], in1=S[1], op=ALU.subtract))
                    P.v(lambda e, S=S: e.tensor_tensor(
                        out=S[3], in0=S[2], in1=S[5], op=ALU.mult))
                    P.v(lambda e, S=S: e.scalar_tensor_tensor(
                        out=S[2], in0=S[3], scalar=1.0 / 64.0, in1=S[1],
                        op0=ALU.mult, op1=ALU.add))
                    P.v(lambda e, S=S, u=u: e.scalar_tensor_tensor(
                        out=S[1], in0=S[2], scalar=256.0, in1=TS[u],
                        op0=ALU.mult, op1=ALU.add))
                    P.rnd(S[2], S[1], S[0], S[3])
                gel_ev = P.mark("vector")
            P.wait("scalar", gel_ev)
            for u in range(2):
                fprev = 30 + u
                ghp = hgel_h[:, fprev * 512:(fprev + 1) * 512]
                P.s(lambda e, ghp=ghp, W=SETS[u][2]: e.activation(
                    out=ghp, in_=W, func=AF.Identity, scale=1.0 / 256.0))
            fl_tok = P.mark("scalar")
            P.wait("vector", fl_tok)
            for u in range(2):
                fprev = 30 + u
                ghp = hgel_h[:, fprev * 512:(fprev + 1) * 512]
                glp = hgel_l[:, fprev * 512:(fprev + 1) * 512]
                P.v(lambda e, ghp=ghp, glp=glp, W=SETS[u][2]:
                    e.scalar_tensor_tensor(
                        out=glp, in0=W, scalar=1.0 / 256.0, in1=ghp,
                        op0=ALU.mult, op1=ALU.subtract))
            tok_hgel = P.mark("vector")
            if s == 0 and tg == 0:
                dbgt("hgel_h", hgel_h[:, 0:2048], [tok_hgel])
                dbgt("xh1_h", xh1_h[:, 0:1024], [tok_hgel])

            P.waits("tensor", [tok_hgel])
            oq_ev = [None, None]
            o_last = None
            for oc in range(8):
                ab = oc % 2
                P.wait("tensor", oq_ev[ab])
                mmq = None
                for q in range(4):
                    hq, wtokq = ws_load(
                        wslice(w2_hs, w2_ls, q * 8, (q + 1) * 8, oc))
                    P.waits("tensor", wtokq)

                    def o_mms(e, hq=hq, ab=ab, q=q):
                        off = hq * 1024
                        last = None
                        for ci in range(8):
                            c = q * 8 + ci
                            wh = ws_h[:, off + ci * 128: off + (ci + 1) * 128]
                            wl = ws_l[:, off + ci * 128: off + (ci + 1) * 128]
                            hh = hgel_h[:, c * 512:(c + 1) * 512]
                            hl = hgel_l[:, c * 512:(c + 1) * 512]
                            e.matmul(pA[ab], wh, hh, start=(c == 0),
                                     stop=False)
                            e.matmul(pA[ab], wh, hl, start=False, stop=False)
                            last = e.matmul(pA[ab], wl, hh, start=False,
                                            stop=(c == 31))
                        return last
                    mmq = P.mark_inst("tensor", o_mms)
                    ws_done(hq, mmq)
                o_last = mmq
                P.wait("scalar", mmq)
                P.s(lambda e, ab=ab, oc=oc: e.activation(
                    out=GB[4], in_=pA[ab], func=AF.Identity, scale=256.0,
                    bias=b2t[:, oc:oc + 1]))
                mso = P.mark("scalar")
                P.wait("vector", mso)
                P.rnd(GB[5], GB[4], GB[6], GB[7])
                P.v(lambda e, oc=oc: e.tensor_scalar(
                    out=ofm[:, oc * 512:(oc + 1) * 512], in0=GB[5],
                    scalar1=1.0 / 256.0, scalar2=None, op0=ALU.mult))
                oq_ev[ab] = P.mark("vector")
            ffn_done = o_last
            tok_ofm = P.mark("vector")

            P.waits("tensor", [tok_ofm])
            for j in range(4):
                for half in range(2):
                    P.wait("tensor", ptev[half])
                    for jo in range(4):
                        oc = half * 4 + jo
                        P.t(lambda e, half=half, jo=jo, oc=oc, j=j:
                            e.transpose(pT[half][:, jo * 128:(jo + 1) * 128],
                                        ofm[:, oc * 512 + j * 128:
                                            oc * 512 + (j + 1) * 128],
                                        identt))
                mt = P.mark("tensor")
                P.wait("vector", mt)
                osl = (j % 2) * 1024
                P.wait("vector", out_dma_toks[j % 2])
                tix = (tg * 4 + j) * 1024
                for half in range(2):
                    P.v(lambda e, half=half, tix=tix, osl=osl:
                        e.scalar_tensor_tensor(
                            out=ot_st[:, osl + half * 512:
                                      osl + (half + 1) * 512],
                            in0=x1h[:, tix + half * 512:
                                    tix + (half + 1) * 512],
                            scalar=1.0, in1=pT[half], op0=ALU.mult,
                            op1=ALU.add))
                    P.v(lambda e, half=half, tix=tix, osl=osl:
                        e.tensor_tensor(
                            out=ot_st[:, osl + half * 512:
                                      osl + (half + 1) * 512],
                            in0=ot_st[:, osl + half * 512:
                                      osl + (half + 1) * 512],
                            in1=x1l[:, tix + half * 512:
                                    tix + (half + 1) * 512],
                            op=ALU.add))
                mv = P.mark("vector")
                ptev[0] = mv
                ptev[1] = mv
                P.wait("sync", mv)
                out_dma_toks[j % 2] = P.dma("sync", out_r[s, tg * 4 + j],
                                            ot_st[:, osl:osl + 1024])

        tok_seq_end = P.mark("tensor")
        tok_out_prev = out_dma_toks[1]

    P.wait("sync", ("d_sync", P.cnt["d_sync"]))
    P.wait("gpsimd", ("d_gpsimd", P.cnt["d_gpsimd"]))

    with stack:
        with nc.Block() as block:
            P.emit(block)

    return nc


def host_inputs(L0_attn_x, L0_attn_w_attn, L0_attn_b_attn, L0_attn_la_attn,
                L0_attn_lb_attn, L0_attn_w_proj, L0_attn_b_proj,
                L0_attn_la_proj, L0_attn_lb_proj, L0_ffn_w1, L0_ffn_b1,
                L0_ffn_w2, L0_ffn_b2):
    C = np.ascontiguousarray
    asn = np.asarray
    wqkvT = C(asn(L0_attn_w_attn).T)
    battn = C(asn(L0_attn_b_attn).reshape(24, 128).T)
    laTa = C(asn(L0_attn_la_attn).T.reshape(8, 128, 32).transpose(1, 0, 2)
             .reshape(128, 256))
    lbTa = C(asn(L0_attn_lb_attn).T)
    wprojT = C(asn(L0_attn_w_proj).T)
    bproj = C(asn(L0_attn_b_proj).reshape(8, 128).T)
    laTp = C(asn(L0_attn_la_proj).T.reshape(8, 128, 32).transpose(1, 0, 2)
             .reshape(128, 256))
    lbTp = C(asn(L0_attn_lb_proj).T)
    w1 = C(asn(L0_ffn_w1))
    b1c = C(asn(L0_ffn_b1).reshape(32, 128).T)
    w2 = C(asn(L0_ffn_w2))
    b2c = C(asn(L0_ffn_b2).reshape(8, 128).T)

    identx = np.eye(128, dtype=np.float32)
    u = np.arange(896)[None, :]
    p = np.arange(128)[:, None]
    import ml_dtypes
    maskx = np.where(u >= p + 384, 0.0, -1e30).astype(ml_dtypes.bfloat16)
    cvec = np.zeros((128, 8), np.float32)
    cvec[:, 0] = 0.25
    cvec[:, 1] = -12.0
    cvec[:, 2] = 1e-5
    cvec[:, 3] = -1024.0

    x = asn(L0_attn_x)
    shared = dict(wqkvT=wqkvT, battn=battn, laTa=laTa, lbTa=lbTa,
                  wprojT=wprojT, bproj=bproj, laTp=laTp, lbTp=lbTp,
                  w1=w1, b1c=b1c, w2=w2, b2c=b2c, identx=identx,
                  maskx=maskx, cvec=cvec)
    in_maps = []
    for core in range(8):
        d = dict(shared)
        d["x"] = C(x[2 * core: 2 * core + 2])
        in_maps.append(d)
    return in_maps


_NC_CACHE = {}


def kernel(**inputs):
    in_maps = host_inputs(**inputs)
    if "nc" not in _NC_CACHE:
        _NC_CACHE["nc"] = build()
    nc = _NC_CACHE["nc"]
    res = run_bass_kernel_spmd(nc, in_maps, core_ids=list(range(8)))
    out = np.concatenate([np.asarray(r["out"]) for r in res.results], axis=0)
    return out.astype(np.float32)


if __name__ == "__main__":
    nc = build()
    n = sum(1 for _ in nc.all_instructions())
    print("instructions:", n)
